# version 28
# baseline (speedup 1.0000x reference)
import os
import numpy as np
import ml_dtypes
from contextlib import ExitStack

import concourse.bass as bass
import concourse.mybir as mybir
from concourse.bass_utils import run_bass_kernel_spmd

F32 = mybir.dt.float32
BF16 = mybir.dt.bfloat16
I32 = mybir.dt.int32
AF = mybir.ActivationFunctionType
ALU = mybir.AluOpType
BF = ml_dtypes.bfloat16

B, S, D = 4, 2048, 2048
H, HD = 16, 128
HID, R = 8192, 16
NQ = 1024
SCALE = 1.0 / np.sqrt(HD)
NH = 512

ENGS = ("sync", "tensor", "vector", "scalar", "gpsimd")
_SIM_SERIAL = os.environ.get("BASS_SIM_FP64") == "1"


class _Sem:
    __slots__ = ("h", "n")

    def __init__(self, h):
        self.h = h
        self.n = 0

    def bump(self, k):
        self.n += k
        return self.n


class _Bld:
    def __init__(self):
        self.q = {e: [] for e in ENGS}
        self.hw = {e: {} for e in ENGS}

    def wait(self, eng, sem, val):
        if sem is None or val is None or val <= 0:
            return
        if self.hw[eng].get(id(sem), 0) >= val:
            return
        self.hw[eng][id(sem)] = val
        self.q[eng].append(lambda E, s=sem, v=val: E.wait_ge(s.h, v))

    def op(self, eng, fn, inc=None, inc_by=1, waits=()):
        for s, v in waits:
            self.wait(eng, s, v)
        if inc is not None:
            tok = inc.bump(inc_by)
            self.q[eng].append(lambda E, f=fn, s=inc, v=inc_by: f(E).then_inc(s.h, v))
            return tok
        self.q[eng].append(lambda E, f=fn: f(E))
        return None

    def dma(self, queue, out, in_, sem, waits=()):
        if _SIM_SERIAL:
            waits = list(waits) + [(sem, sem.n)]
        return self.op(queue, lambda E, o=out, i=in_: E.dma_start(out=o, in_=i),
                       inc=sem, inc_by=16, waits=waits)

    def flush(self, block):
        for eng in ENGS:
            ops = self.q[eng]
            if not ops:
                continue
            self.q[eng] = []

            def body(E, ops=ops):
                for f in ops:
                    f(E)
            getattr(block, eng)(body)


def _seg(qs):
    if qs < 512:
        return [(qs, 512 - qs), (512, 512)]
    return [(qs, 1024 - qs)]


def build_nc(dbg=()):
    nc = bass.Bass("TRN2", target_bir_lowering=False)
    dt = nc.dram_tensor
    xfullT = dt("xfullT", [16, 128, S], F32, kind="ExternalInput").ap()
    xqT = dt("xqT", [16, 128, NQ], F32, kind="ExternalInput").ap()
    wq_st = dt("wq_st", [16, 128, 16, 128], F32, kind="ExternalInput").ap()
    wk_st = dt("wk_st", [16, 128, 16, 128], F32, kind="ExternalInput").ap()
    wv_st = dt("wv_st", [8, 2, 128, 2048], F32, kind="ExternalInput").ap()
    wo_st = dt("wo_st", [16, 128, 16, 128], F32, kind="ExternalInput").ap()
    wupq_st = dt("wupq_st", [64, 128, 2048], I32, kind="ExternalInput").ap()
    upsc_st = dt("upsc_st", [4, 32, 2048], F32, kind="ExternalInput").ap()
    bup_st = dt("bup_st", [128, 64], F32, kind="ExternalInput").ap()
    laup_st = dt("laup_st", [128, 256], F32, kind="ExternalInput").ap()
    lbup_st = dt("lbup_st", [4, 16, 2048], F32, kind="ExternalInput").ap()
    wdnq_st = dt("wdnq_st", [16, 128, 8192], I32, kind="ExternalInput").ap()
    dnsc_st = dt("dnsc_st", [2, 2, 32, 2048], F32, kind="ExternalInput").ap()
    esel_st = dt("esel_st", [64, 4096], BF16, kind="ExternalInput").ap()
    bdn_st = dt("bdn_st", [128, 16], F32, kind="ExternalInput").ap()
    ladn_st = dt("ladn_st", [128, 1024], F32, kind="ExternalInput").ap()
    lbdn_st = dt("lbdn_st", [16, 2048], F32, kind="ExternalInput").ap()
    ck_st = dt("ck_st", [128, S], BF16, kind="ExternalInput").ap()
    sk_st = dt("sk_st", [128, S], BF16, kind="ExternalInput").ap()
    cq_st = dt("cq_st", [128, NQ], BF16, kind="ExternalInput").ap()
    sq_st = dt("sq_st", [128, NQ], BF16, kind="ExternalInput").ap()
    mask_st = dt("mask_st", [128, 16, 128], BF16, kind="ExternalInput").ap()
    rotm_st = dt("rotm_st", [128, 128], BF16, kind="ExternalInput").ap()
    sel_st = dt("sel_st", [2, 128], F32, kind="ExternalInput").ap()

    out_d = dt("out", [D, NQ], F32, kind="ExternalOutput").ap()
    s_bnc = dt("s_bnc", [16, 128, NQ], BF16,
               kind="ExternalOutput" if "s" in dbg else "Internal").ap()
    dbg_t = {}
    for name, shape in [("xn", [128, 16 * S]), ("xnq", [128, 16 * NQ]),
                        ("kfm0", [128, S]), ("qfm0", [128, NQ]),
                        ("att", [128, 16 * NQ]), ("x2", [128, 32 * NH]),
                        ("g", [128, NH]), ("wup0", [128, 2048]),
                        ("wdn0", [128, 4096]), ("t2r", [16, NH]),
                        ("vtm0", [128, 4096]), ("rc0", [128, NQ]),
                        ("kfm7", [128, S]), ("ptb0", [128, NQ])]:
        if name in dbg:
            dbg_t[name] = dt("dbg_" + name, shape, BF16, kind="ExternalOutput").ap()
    if "y2a" in dbg:
        dbg_t["y2a"] = dt("dbg_y2a", [128, 16 * NH], F32, kind="ExternalOutput").ap()

    bld = _Bld()
    es = ExitStack()
    E = es.enter_context

    SW = _Sem(E(nc.semaphore("SW")))
    SX = _Sem(E(nc.semaphore("SX")))
    SG = _Sem(E(nc.semaphore("SG")))
    SPE = _Sem(E(nc.semaphore("SPE")))
    SACT = _Sem(E(nc.semaphore("SACT")))
    SDVE = _Sem(E(nc.semaphore("SDVE")))
    SPOOL = _Sem(E(nc.semaphore("SPOOL")))

    xnq = E(nc.sbuf_tensor("xnq", [128, 16 * NQ], BF16))
    ones_col = E(nc.sbuf_tensor("ones_col", [128, 1], BF16))
    ones_row = E(nc.sbuf_tensor("ones_row", [1, 128], F32))
    sel_sb = E(nc.sbuf_tensor("sel_sb", [2, 128], F32))
    bup_sb = E(nc.sbuf_tensor("bup_sb", [128, 64], F32))
    bdn_sb = E(nc.sbuf_tensor("bdn_sb", [128, 16], F32))
    c_eps = E(nc.sbuf_tensor("c_eps", [1, 1], F32))
    c_m75 = E(nc.sbuf_tensor("c_m75", [128, 1], F32))

    with nc.Block() as block:
        bld.dma("scalar", sel_sb[:], sel_st, SX)
        bld.dma("scalar", bup_sb[:], bup_st, SX)
        bld.dma("scalar", bdn_sb[:], bdn_st, SX)
        tok_c0 = SX.n
        bld.op("vector", lambda E: E.memset(ones_col[:], 1.0))
        bld.op("vector", lambda E: E.memset(c_eps[:], 1e-5))
        bld.op("vector", lambda E: E.memset(c_m75[:], -7.5))
        tok_ones = bld.op("vector", lambda E: E.memset(ones_row[:], 1.0), inc=SDVE,
                          waits=[(SX, tok_c0)])

        att_es = ExitStack()
        A = att_es.enter_context
        xn = A(nc.sbuf_tensor("xn", [128, 16 * S], BF16))
        ck_sb = A(nc.sbuf_tensor("ck_sb", [128, S], BF16))
        sk_sb = A(nc.sbuf_tensor("sk_sb", [128, S], BF16))
        cq_sb = A(nc.sbuf_tensor("cq_sb", [128, NQ], BF16))
        sq_sb = A(nc.sbuf_tensor("sq_sb", [128, NQ], BF16))
        mask_sb = A(nc.sbuf_tensor("mask_sb", [128, 16 * 128], BF16))
        rotm_sb = A(nc.sbuf_tensor("rotm_sb", [128, 128], BF16))
        att_all = A(nc.sbuf_tensor("att_all", [128, 16 * NQ], BF16))

        bld.dma("scalar", ck_sb[:], ck_st, SX)
        bld.dma("scalar", sk_sb[:], sk_st, SX)
        bld.dma("scalar", cq_sb[:], cq_st, SX)
        bld.dma("scalar", sq_sb[:], sq_st, SX)
        bld.dma("scalar", mask_sb[:].rearrange("p (k q) -> p k q", k=16), mask_st, SX)
        tok_cst = bld.dma("scalar", rotm_sb[:], rotm_st, SX)

        ln_es = ExitStack()
        L = ln_es.enter_context
        xst = [L(nc.sbuf_tensor(f"xst{i}", [128, S], F32)) for i in range(2)]
        xsq = [L(nc.sbuf_tensor(f"xsq{i}", [128, S], BF16)) for i in range(2)]
        mean_r = L(nc.sbuf_tensor("mean_r", [1, S], F32))
        m2_r = L(nc.sbuf_tensor("m2_r", [1, S], F32))
        rstd_r = L(nc.sbuf_tensor("rstd_r", [1, S], F32))
        meanq_r = L(nc.sbuf_tensor("meanq_r", [1, NQ], F32))
        m2q_r = L(nc.sbuf_tensor("m2q_r", [1, NQ], F32))
        rstdq_r = L(nc.sbuf_tensor("rstdq_r", [1, NQ], F32))

        ln_ps1 = ExitStack()
        pstat = ln_ps1.enter_context(nc.psum_tensor("pstat", [64, S], F32))
        pstatq = ln_ps1.enter_context(nc.psum_tensor("pstatq", [64, NQ], F32))

        tok_xbf, tok_xsq, tok_sq_mm = {}, {}, {}
        plans = [("f", xfullT, S, 4, xn, pstat), ("q", xqT, NQ, 2, xnq, pstatq)]
        for pi, (tag, src, width, ncols, dst, pst) in enumerate(plans):
            for dc in range(16):
                w = []
                pk = (tag, dc - 2) if dc >= 2 else (("f", 14 + dc) if pi == 1 else None)
                if pk is not None:
                    w = [(SDVE, tok_xbf[pk]), (SACT, tok_xsq[pk])]
                t_dma = bld.dma("sync", xst[dc % 2][:, 0:width], src[dc], SW,
                                waits=w)
                sl = dst[:, dc * width:(dc + 1) * width]
                tok_xbf[(tag, dc)] = bld.op(
                    "vector", lambda E, o=sl, i=xst[dc % 2][:, 0:width]:
                    E.tensor_copy(o, i), inc=SDVE, waits=[(SW, t_dma)])
                wsq = [(SW, t_dma)]
                if pk is not None:
                    wsq.append((SPE, tok_sq_mm[pk]))
                tok_xsq[(tag, dc)] = bld.op(
                    "scalar", lambda E, o=xsq[dc % 2][:, 0:width],
                    i=xst[dc % 2][:, 0:width]: E.activation(o, i, AF.Square),
                    inc=SACT, waits=wsq)
                for ts in range(ncols):
                    bld.op("tensor", lambda E, o=pst[0:1, ts * 512:(ts + 1) * 512],
                           l=ones_col[:], r=sl[:, ts * 512:(ts + 1) * 512],
                           s=(dc == 0): E.matmul(o, l, r, start=s, stop=False,
                                                 skip_group_check=True),
                           waits=[(SDVE, tok_xbf[(tag, dc)]), (SDVE, tok_ones)])
                last = None
                for ts in range(ncols):
                    last = bld.op(
                        "tensor", lambda E, o=pst[32:33, ts * 512:(ts + 1) * 512],
                        l=ones_col[:], r=xsq[dc % 2][:, ts * 512:(ts + 1) * 512],
                        s=(dc == 0): E.matmul(o, l, r, start=s, stop=False,
                                              skip_group_check=True),
                        inc=SPE, waits=[(SACT, tok_xsq[(tag, dc)])])
                tok_sq_mm[(tag, dc)] = last
        tok_stats = SPE.n
        for pst, n, mr, m2r, rsr in ((pstat, S, mean_r, m2_r, rstd_r),
                                     (pstatq, NQ, meanq_r, m2q_r, rstdq_r)):
            ta = bld.op("scalar", lambda E, o=mr[:], i=pst[0:1, 0:n]:
                        E.activation(o, i, AF.Copy, scale=1.0 / D), inc=SACT,
                        waits=[(SPE, tok_stats)])
            bld.op("vector", lambda E, o=m2r[:], a=mr[:]: E.tensor_mul(o, a, a),
                   waits=[(SACT, ta)])
            bld.op("vector", lambda E, o=rsr[:], i=pst[32:33, 0:n]:
                   E.tensor_scalar(o, i, 1.0 / D, None, ALU.mult))
            tv = bld.op("vector", lambda E, o=m2r[:], a=rsr[:], m=m2r[:]:
                        E.tensor_tensor(o, a, m, ALU.subtract), inc=SDVE)
            bld.op("scalar", lambda E, o=m2r[:], i=m2r[:]:
                   E.activation(o, i, AF.Ln, bias=c_eps[:]),
                   waits=[(SDVE, tv), (SDVE, tok_ones)])
            bld.op("scalar", lambda E, o=rsr[:], i=m2r[:]:
                   E.activation(o, i, AF.Exp, scale=-0.5), inc=SACT)
        tok_rows_a = SACT.n
        tok_rows_d = SDVE.n
        bld.flush(block)
        ln_ps1.close()
        ln_ps2 = ExitStack()
        pmbc = ln_ps2.enter_context(nc.psum_tensor("pmbc", [128, S], F32))
        prbc = ln_ps2.enter_context(nc.psum_tensor("prbc", [128, S], F32))
        tok_norm_d = tok_rows_d
        for tag, n, ncols, mr, rsr, dst in (("f", S, 4, mean_r, rstd_r, xn),
                                            ("q", NQ, 2, meanq_r, rstdq_r, xnq)):
            last = None
            for ts in range(ncols):
                bld.op("tensor", lambda E, o=pmbc[:, ts * 512:(ts + 1) * 512],
                       l=ones_row[:], r=mr[0:1, ts * 512:(ts + 1) * 512]:
                       E.matmul(o, l, r, start=True, stop=True, skip_group_check=True),
                       waits=[(SACT, tok_rows_a), (SDVE, tok_norm_d)])
                last = bld.op("tensor", lambda E, o=prbc[:, ts * 512:(ts + 1) * 512],
                              l=ones_row[:], r=rsr[0:1, ts * 512:(ts + 1) * 512]:
                              E.matmul(o, l, r, start=True, stop=True,
                                       skip_group_check=True), inc=SPE)
            tok_bc = last
            for dc in range(16):
                sl = dst[:, dc * n:(dc + 1) * n]
                stg = xst[dc % 2][:, 0:n]
                bld.op("vector", lambda E, o=stg, a=sl, m=pmbc[:, 0:n]:
                       E.tensor_tensor(o, a, m, ALU.subtract),
                       waits=[(SPE, tok_bc), (SACT, tok_rows_a)])
                bld.op("vector", lambda E, o=sl, a=stg, r=prbc[:, 0:n]:
                       E.tensor_tensor(o, a, r, ALU.mult), inc=SDVE)
            tok_norm_d = SDVE.n
        tok_ln_d = SDVE.n
        tok_ln_a = SACT.n
        if "xn" in dbg:
            bld.dma("gpsimd", dbg_t["xn"], xn[:], SG, waits=[(SDVE, tok_ln_d)])
        if "xnq" in dbg:
            bld.dma("gpsimd", dbg_t["xnq"], xnq[:], SG, waits=[(SDVE, tok_ln_d)])
        bld.flush(block)
        ln_ps2.close()
        ln_es.close()
        tok_ln_pe = SPE.n

        hd_es = ExitStack()
        HA = hd_es.enter_context
        v_tm = HA(nc.sbuf_tensor("v_tm", [128, 16 * 256], BF16))
        wv_bf = HA(nc.sbuf_tensor("wv_bf", [128, 4096], BF16))
        wraw = [HA(nc.sbuf_tensor(f"wraw{i}", [128, 2048], F32)) for i in range(2)]
        wr_i = [0]
        wk_bf = wv_bf[:, 0:2048]
        wq_bf = wv_bf[:, 2048:4096]
        kq_bf = HA(nc.sbuf_tensor("kq_bf", [128, 2048], BF16))
        scrA = HA(nc.sbuf_tensor("scrA", [128, 2048], BF16))
        k_fm = HA(nc.sbuf_tensor("k_fm", [128, 2048], BF16))
        q_fm = HA(nc.sbuf_tensor("q_fm", [128, NQ], BF16))
        ptb = [HA(nc.sbuf_tensor(f"ptb{i}", [128, NQ], BF16)) for i in range(3)]
        rc_f32 = HA(nc.sbuf_tensor("rc_f32", [128, NQ], F32))
        sums_sb = HA(nc.sbuf_tensor("sums_sb", [1, NQ], F32))
        pK = HA(nc.psum_tensor("pK", [128, 2048], F32))
        pQ = HA(nc.psum_tensor("pQ", [128, NQ], F32))
        pS = HA(nc.psum_tensor("pS", [128, NQ], F32))

        wraw_free = [[(SDVE, tok_ln_d), (SPE, tok_ln_pe), (SACT, tok_ln_a)],
                     [(SDVE, tok_ln_d), (SPE, tok_ln_pe), (SACT, tok_ln_a)]]
        rc_prev_gate = []
        prev_qproj = None
        v_last_pe = None
        attev_prev = None
        sums_prev = None
        rc_prev = None
        av_prev = None

        for g2 in range(8):
            cv_tok = None
            for sub in range(2):
                wi = wr_i[0] % 2
                wr_i[0] += 1
                t_dma = bld.dma("sync", wraw[wi][:], wv_st[g2, sub], SW,
                                waits=wraw_free[wi])
                wcv_v = [(SW, t_dma)]
                if prev_qproj is not None:
                    wcv_v.append((SPE, prev_qproj))
                cv_tok = bld.op("scalar",
                                lambda E, o=wv_bf[:, sub * 2048:(sub + 1) * 2048],
                                i=wraw[wi][:]: E.activation(o, i, AF.Copy),
                                inc=SACT, waits=wcv_v)
                wraw_free[wi] = [(SACT, cv_tok)]
            vp_act = {}
            for tc in range(16):
                pv = pS[:, 0:256] if tc % 2 == 0 else pS[:, 512:768]
                w = [(SACT, cv_tok), (SDVE, tok_ln_d)]
                if attev_prev is not None:
                    w += [(SDVE, attev_prev), (SACT, rc_prev)]
                if tc >= 2:
                    w.append((SACT, vp_act[tc - 2]))
                last = None
                for dc in range(16):
                    last = bld.op(
                        "tensor", lambda E, o=pv,
                        l=xn[:, dc * S + tc * 128: dc * S + tc * 128 + 128],
                        r=wv_bf[:, dc * 256:(dc + 1) * 256],
                        s=(dc == 0), p=(dc == 15): E.matmul(o, l, r, start=s, stop=p),
                        inc=SPE if dc == 15 else None, waits=w)
                vp_act[tc] = bld.op(
                    "scalar", lambda E, o=v_tm[:, tc * 256:(tc + 1) * 256], i=pv:
                    E.activation(o, i, AF.Copy), inc=SACT, waits=[(SPE, last)])
                v_last_pe = last
            tok_v_done = vp_act[15]
            if g2 == 0 and "vtm0" in dbg:
                tvt = bld.dma("gpsimd", dbg_t["vtm0"], v_tm[:], SG,
                              waits=[(SACT, tok_v_done)])
                wraw_free = wraw_free + [(SG, tvt)]

            for hh in range(2):
                h = 2 * g2 + hh
                wi = wr_i[0] % 2
                wr_i[0] += 1
                t_dma = bld.dma("sync", wraw[wi][:], wk_st[h], SW,
                                waits=wraw_free[wi])
                cvk = bld.op("vector", lambda E, o=wk_bf, i=wraw[wi][:]:
                             E.tensor_copy(o, i), inc=SDVE,
                             waits=[(SW, t_dma), (SPE, v_last_pe)])
                wraw_free[wi] = [(SDVE, cvk)]
                w_k = [(SDVE, cvk)]
                if attev_prev is not None:
                    w_k += [(SDVE, attev_prev), (SACT, sums_prev)]
                tok_kproj = None
                for ts in range(4):
                    for dc in range(16):
                        tok_kproj = bld.op(
                            "tensor", lambda E, o=pK[:, ts * 512:(ts + 1) * 512],
                            l=wv_bf[:, dc * 128:dc * 128 + 128],
                            r=xn[:, dc * S + ts * 512: dc * S + ts * 512 + 512],
                            s=(dc == 0), p=(dc == 15):
                            E.matmul(o, l, r, start=s, stop=p),
                            inc=SPE if (dc == 15 and ts == 3) else None, waits=w_k)
                tok_kbf = bld.op("scalar", lambda E, o=kq_bf[:], i=pK[:]:
                                 E.activation(o, i, AF.Copy), inc=SACT,
                                 waits=[(SPE, tok_kproj)])
                tok_rotk = None
                for ts in range(4):
                    tok_rotk = bld.op(
                        "tensor", lambda E, o=pK[:, ts * 512:(ts + 1) * 512],
                        l=rotm_sb[:], r=kq_bf[:, ts * 512:(ts + 1) * 512]:
                        E.matmul(o, l, r, start=True, stop=True),
                        inc=SPE if ts == 3 else None,
                        waits=[(SACT, tok_kbf), (SX, tok_cst)])
                bld.op("vector", lambda E, o=scrA[:], a=kq_bf[:], b=ck_sb[:]:
                       E.tensor_tensor(o, a, b, ALU.mult), waits=[(SACT, tok_kbf)])
                tok_kfm = None
                for hf in range(2):
                    sl = slice(hf * 1024, (hf + 1) * 1024)
                    bld.op("vector", lambda E, o=ptb[hf][:], a=pK[:, sl],
                           b=sk_sb[:, sl]: E.tensor_tensor(o, a, b, ALU.mult),
                           waits=[(SPE, tok_rotk)])
                    tok_kfm = bld.op("vector", lambda E, o=k_fm[:, sl],
                                     a=scrA[:, sl], b=ptb[hf][:]:
                                     E.tensor_tensor(o, a, b, ALU.add),
                                     inc=SDVE if hf == 1 else None)
                wi = wr_i[0] % 2
                wr_i[0] += 1
                t_dma = bld.dma("sync", wraw[wi][:], wq_st[h], SW,
                                waits=wraw_free[wi])
                cvq = bld.op("vector", lambda E, o=wq_bf, i=wraw[wi][:]:
                             E.tensor_copy(o, i), inc=SDVE,
                             waits=[(SW, t_dma), (SPE, tok_kproj)])
                wraw_free[wi] = [(SDVE, cvq)]
                w_q = [(SDVE, cvq)]
                if rc_prev is not None:
                    w_q.append((SACT, rc_prev))
                w_q += rc_prev_gate
                rc_prev_gate = []
                tok_qproj = None
                for ts in range(2):
                    for dc in range(16):
                        tok_qproj = bld.op(
                            "tensor", lambda E, o=pQ[:, ts * 512:(ts + 1) * 512],
                            l=wv_bf[:, 2048 + dc * 128:2048 + dc * 128 + 128],
                            r=xnq[:, dc * NQ + ts * 512: dc * NQ + ts * 512 + 512],
                            s=(dc == 0), p=(dc == 15):
                            E.matmul(o, l, r, start=s, stop=p),
                            inc=SPE if (dc == 15 and ts == 1) else None, waits=w_q)
                tok_qbf = bld.op("scalar", lambda E, o=kq_bf[:, 0:NQ], i=pQ[:]:
                                 E.activation(o, i, AF.Copy), inc=SACT,
                                 waits=[(SPE, tok_qproj), (SDVE, tok_kfm)])
                tok_rotq = None
                for ts in range(2):
                    tok_rotq = bld.op(
                        "tensor", lambda E, o=pQ[:, ts * 512:(ts + 1) * 512],
                        l=rotm_sb[:], r=kq_bf[:, ts * 512:(ts + 1) * 512]:
                        E.matmul(o, l, r, start=True, stop=True),
                        inc=SPE if ts == 1 else None, waits=[(SACT, tok_qbf)])
                prev_qproj = tok_qproj
                bld.op("vector", lambda E, o=scrA[:, 0:NQ], a=kq_bf[:, 0:NQ],
                       b=cq_sb[:]: E.tensor_tensor(o, a, b, ALU.mult),
                       waits=[(SACT, tok_qbf)])
                bld.op("vector", lambda E, o=ptb[0][:], a=pQ[:], b=sq_sb[:]:
                       E.tensor_tensor(o, a, b, ALU.mult), waits=[(SPE, tok_rotq)])
                tok_qfm = bld.op("vector", lambda E, o=q_fm[:], a=scrA[:, 0:NQ],
                                 b=ptb[0][:]: E.tensor_tensor(o, a, b, ALU.add),
                                 inc=SDVE)
                if h == 0 and "kfm0" in dbg:
                    bld.dma("gpsimd", dbg_t["kfm0"], k_fm[:], SG,
                            waits=[(SDVE, tok_kfm)])
                if h == 0 and "qfm0" in dbg:
                    bld.dma("gpsimd", dbg_t["qfm0"], q_fm[:], SG,
                            waits=[(SDVE, tok_qfm)])
                exp_t, mask_t, av_t = {}, {}, {}
                sums_last = None
                for step in range(18):
                    if step < 16:
                        kb = step
                        qs = 128 * (kb // 2)
                        N = NQ - qs
                        psx = pS if kb % 2 == 0 else pQ
                        w = [(SDVE, tok_qfm), (SDVE, tok_kfm), (SACT, tok_v_done)]
                        if kb >= 2:
                            w.append((SACT, exp_t[kb - 2]))
                        elif rc_prev is not None:
                            w.append((SACT, rc_prev))
                        sc_tok = None
                        segs = ((0, N),) if N <= 512 else ((0, 512), (512, N - 512))
                        for off, n in segs:
                            sc_tok = bld.op(
                                "tensor", lambda E, o=psx[:, off:off + n],
                                l=k_fm[:, kb * 128:(kb + 1) * 128],
                                r=q_fm[:, qs + off: qs + off + n]:
                                E.matmul(o, l, r, start=True, stop=True,
                                         skip_group_check=True),
                                inc=SPE if off + n == N else None, waits=w)
                        we = [(SPE, sc_tok)]
                        if kb >= 3:
                            we.append((SPE, av_t[kb - 3]))
                        elif av_prev is not None:
                            we.append((SPE, av_prev))
                        exp_t[kb] = bld.op(
                            "scalar", lambda E, o=ptb[kb % 3][:, qs:NQ],
                            i=psx[:, 0:N]: E.activation(o, i, AF.Exp), inc=SACT,
                            waits=we)
                        mask_t[kb] = bld.op(
                            "gpsimd", lambda E, o=ptb[kb % 3][:, qs:qs + 128],
                            m=mask_sb[:, kb * 128:(kb + 1) * 128]:
                            E.tensor_tensor(o, o, m, ALU.mult), inc=SPOOL,
                            waits=[(SACT, exp_t[kb])])
                    if step >= 2:
                        kb = step - 2
                        qs = 128 * (kb // 2)
                        w = [(SPOOL, mask_t[kb])]
                        for off, n in _seg(qs):
                            sums_last = bld.op(
                                "tensor",
                                lambda E, o=pK[0:1, 1024 + off: 1024 + off + n],
                                l=ones_col[:], r=ptb[kb % 3][:, off:off + n],
                                s=(kb == 0): E.matmul(o, l, r, start=s,
                                                      stop=(kb == 15),
                                                      skip_group_check=True),
                                inc=SPE if kb == 15 else None, waits=w)
                        for off, n in _seg(qs):
                            av_t[kb] = bld.op(
                                "tensor", lambda E, o=pK[:, off:off + n],
                                l=v_tm[:, kb * 256 + hh * 128:
                                       kb * 256 + hh * 128 + 128],
                                r=ptb[kb % 3][:, off:off + n],
                                s=(kb == 0): E.matmul(o, l, r, start=s,
                                                      stop=(kb == 15),
                                                      skip_group_check=True),
                                inc=SPE if kb == 15 else None, waits=w)
                t_sums = bld.op("scalar", lambda E, o=sums_sb[:],
                                i=pK[0:1, 1024:2048]: E.activation(o, i, AF.Copy),
                                inc=SACT, waits=[(SPE, sums_last)])
                t_bc = None
                for ts in range(2):
                    t_bc = bld.op(
                        "tensor", lambda E, o=pQ[:, ts * 512:(ts + 1) * 512],
                        l=ones_row[:], r=sums_sb[0:1, ts * 512:(ts + 1) * 512]:
                        E.matmul(o, l, r, start=True, stop=True,
                                 skip_group_check=True),
                        inc=SPE if ts == 1 else None,
                        waits=[(SACT, t_sums), (SACT, exp_t[15]), (SPE, av_t[15])])
                bld.op("scalar", lambda E, o=rc_f32[:], i=pQ[:]:
                       E.activation(o, i, AF.Ln), waits=[(SPE, t_bc)])
                t_rc = bld.op("scalar", lambda E, o=rc_f32[:], i=rc_f32[:]:
                              E.activation(o, i, AF.Exp, scale=-1.0), inc=SACT)
                attev_prev = bld.op(
                    "vector", lambda E, o=att_all[:, h * NQ:(h + 1) * NQ],
                    a=pK[:, 0:NQ], b=rc_f32[:]: E.tensor_tensor(o, a, b, ALU.mult),
                    inc=SDVE, waits=[(SACT, t_rc)])
                if h == 0 and "rc0" in dbg:
                    trc = bld.dma("gpsimd", dbg_t["rc0"], rc_f32[:], SG,
                                  waits=[(SACT, t_rc)])
                    rc_prev_gate.append((SG, trc))
                if h == 0 and "ptb0" in dbg:
                    tpb = bld.dma("gpsimd", dbg_t["ptb0"], ptb[15 % 3][:], SG,
                                  waits=[(SPE, av_t[15])])
                    rc_prev_gate.append((SG, tpb))
                if h == 7 and "kfm7" in dbg:
                    bld.dma("gpsimd", dbg_t["kfm7"], k_fm[:], SG,
                            waits=[(SDVE, tok_kfm)])
                sums_prev = t_sums
                rc_prev = t_rc
                av_prev = av_t[15]
        tok_att_d = attev_prev
        tok_att_a = SACT.n
        tok_att_p = SPE.n
        tok_att_pl = SPOOL.n
        dbg_gate = []
        if "att" in dbg:
            tat = bld.dma("gpsimd", dbg_t["att"], att_all[:], SG,
                          waits=[(SDVE, tok_att_d)])
            dbg_gate.append((SG, tat))
        bld.flush(block)
        hd_es.close()

        wo_es = ExitStack()
        WA = wo_es.enter_context
        woraw = [WA(nc.sbuf_tensor(f"woraw{i}", [128, 2048], F32)) for i in range(2)]
        wo_bf = [WA(nc.sbuf_tensor(f"wo_bf{i}", [128, 2048], BF16)) for i in range(2)]
        s_tile = [WA(nc.sbuf_tensor(f"s_tile{i}", [128, NQ], BF16)) for i in range(2)]
        pW = [WA(nc.psum_tensor(f"pW{i}", [128, NQ], F32)) for i in range(2)]

        wo_guard = [(SDVE, tok_att_d), (SACT, tok_att_a), (SPE, tok_att_p),
                    (SPOOL, tok_att_pl)]
        wo_cv, wo_ev, wo_dma, wo_mm = {}, {}, {}, {}
        for fc in range(16):
            w = list(wo_guard) if fc < 2 else [(SACT, wo_cv[fc - 2])]
            t_dma = bld.dma("sync", woraw[fc % 2][:], wo_st[fc], SW, waits=w)
            wcv = [(SW, t_dma)] + (list(wo_guard) if fc < 2 else
                                   [(SPE, wo_mm[fc - 2])])
            wo_cv[fc] = bld.op("scalar", lambda E, o=wo_bf[fc % 2][:],
                               i=woraw[fc % 2][:]: E.activation(o, i, AF.Copy),
                               inc=SACT, waits=wcv)
            w_mm = [(SACT, wo_cv[fc]), (SDVE, tok_att_d)]
            if fc >= 2:
                w_mm.append((SACT, wo_ev[fc - 2]))
            last = None
            for ts in range(2):
                for dc in range(16):
                    last = bld.op(
                        "tensor", lambda E, o=pW[fc % 2][:, ts * 512:(ts + 1) * 512],
                        l=wo_bf[fc % 2][:, dc * 128:(dc + 1) * 128],
                        r=att_all[:, dc * NQ + ts * 512: dc * NQ + ts * 512 + 512],
                        s=(dc == 0), p=(dc == 15): E.matmul(o, l, r, start=s, stop=p),
                        inc=SPE if (ts == 1 and dc == 15) else None, waits=w_mm)
            wo_mm[fc] = last
            wev = [(SPE, last)]
            if fc >= 2:
                wev.append((SG, wo_dma[fc - 2]))
            wo_ev[fc] = bld.op("scalar", lambda E, o=s_tile[fc % 2][:],
                               i=pW[fc % 2][:]: E.activation(o, i, AF.Copy),
                               inc=SACT, waits=wev)
            wo_dma[fc] = bld.dma("gpsimd", s_bnc[fc], s_tile[fc % 2][:], SG,
                                 waits=[(SACT, wo_ev[fc])])
        tok_wo_p = SPE.n
        tok_wo_a = SACT.n
        tok_wo_pl = SPOOL.n
        tok_wo_g = SG.n
        bld.flush(block)
        wo_es.close()
        att_es.close()

        ffn_es = ExitStack()
        FA = ffn_es.enter_context
        x2g = FA(nc.sbuf_tensor("x2g", [128, 32 * NH], BF16))
        y2acc = FA(nc.sbuf_tensor("y2acc", [128, 16 * NH], BF16))
        raw = [FA(nc.sbuf_tensor(f"raw{i}", [128, 4096], I32)) for i in range(2)]
        upraw = [raw[0][:, 0:2048], raw[1][:, 0:2048]]
        scst = FA(nc.sbuf_tensor("scst", [32, 2048], F32))
        scb = FA(nc.sbuf_tensor("scb", [64, 2048], BF16))
        esel = FA(nc.sbuf_tensor("esel", [64, 4096], BF16))
        wup_bf = [FA(nc.sbuf_tensor(f"wup_bf{i}", [128, 2048], BF16)) for i in range(2)]
        laup_bf = FA(nc.sbuf_tensor("laup_bf", [128, 256], BF16))
        lbup_bf = FA(nc.sbuf_tensor("lbup_bf", [16, 8192], BF16))
        ladn_bf = FA(nc.sbuf_tensor("ladn_bf", [128, 1024], BF16))
        lbdn_bf = FA(nc.sbuf_tensor("lbdn_bf", [16, 2048], BF16))
        t1_bf = FA(nc.sbuf_tensor("t1_bf", [16, NH], BF16))
        t2_bf = FA(nc.sbuf_tensor("t2_bf", [16, NH], BF16))
        g_tile = [FA(nc.sbuf_tensor(f"g_tile{i}", [128, NH], BF16)) for i in range(2)]
        sg_t = [FA(nc.sbuf_tensor(f"sg_t{i}", [128, NH], BF16)) for i in range(2)]
        xq_t = [FA(nc.sbuf_tensor(f"xq_t{i}", [128, NH], F32)) for i in range(2)]
        out_t = [FA(nc.sbuf_tensor(f"out_t{i}", [128, NH], F32)) for i in range(2)]
        pBc = FA(nc.psum_tensor("pBc", [128, 2048], F32))
        pY = [FA(nc.psum_tensor(f"pY{i}", [128, 512], F32)) for i in range(2)]
        pT = FA(nc.psum_tensor("pT", [16, 512], F32))

        guard = [(SPE, tok_wo_p), (SACT, tok_wo_a), (SPOOL, tok_wo_pl),
                 (SDVE, tok_att_d), (SG, tok_wo_g)] + dbg_gate
        t_e = bld.dma("scalar", esel[:], esel_st, SX, waits=guard)
        cv_esel = t_e
        lora_es = ExitStack()
        laup_f = lora_es.enter_context(nc.sbuf_tensor("laup_f", [128, 256], F32))
        lb_f = lora_es.enter_context(nc.sbuf_tensor("lb_f", [16, 2048], F32))
        ladn_f = lora_es.enter_context(nc.sbuf_tensor("ladn_f", [128, 1024], F32))
        t_d = bld.dma("scalar", laup_f[:], laup_st, SX, waits=guard)
        cv_laup = bld.op("scalar", lambda E, o=laup_bf[:], i=laup_f[:]:
                         E.activation(o, i, AF.Copy), inc=SACT,
                         waits=guard + [(SX, t_d)])
        cv_lbup = cv_laup
        for i in range(4):
            t_d = bld.dma("scalar", lb_f[:], lbup_st[i], SX,
                          waits=[(SACT, cv_lbup)])
            cv_lbup = bld.op("scalar",
                             lambda E, o=lbup_bf[:, i * 2048:(i + 1) * 2048],
                             i_=lb_f[:]: E.activation(o, i_, AF.Copy), inc=SACT,
                             waits=[(SX, t_d)])
        t_d = bld.dma("scalar", ladn_f[:], ladn_st, SX, waits=guard)
        cv_ladn = bld.op("scalar", lambda E, o=ladn_bf[:], i=ladn_f[:]:
                         E.activation(o, i, AF.Copy), inc=SACT, waits=[(SX, t_d)])
        t_d = bld.dma("scalar", lb_f[:], lbdn_st, SX, waits=[(SACT, cv_lbup)])
        cv_lbdn = bld.op("scalar", lambda E, o=lbdn_bf[:], i=lb_f[:]:
                         E.activation(o, i, AF.Copy), inc=SACT, waits=[(SX, t_d)])
        bld.flush(block)
        lora_es.close()
        dnraw = [raw[0][:], raw[1][:]]
        wdn_bf = [FA(nc.sbuf_tensor(f"wdn_bf{i}", [128, 4096], BF16)) for i in range(2)]

        py_cons = [None, None]
        raw_act = [None, None]
        pbc_dve = [None, None]
        wup_dve = [None, None]
        sc_cast = None
        scb_pe = None
        wup_pe = [None, None]

        wdn_pe = [None, None]
        wdn_deq = [None, None]
        dn_bc_last = None
        t1_act = None
        t2_evac = None
        sil_t, odve_t, outd_t, relu_t, y2_t = {}, {}, {}, {}, {}
        pend_tail = [None]
        dn_dbg_gate = []

        for t2 in range(2):
            if pend_tail[0] is not None:
                pend_tail[0]()
                pend_tail[0] = None
            last = None
            for dc in range(16):
                wl = list(guard) + [(SACT, cv_laup)]
                if t2_evac is not None:
                    wl.append((SACT, t2_evac))
                last = bld.op(
                    "tensor", lambda E, o=pT[0:16, 0:512],
                    l=laup_bf[:, dc * 16:(dc + 1) * 16],
                    r=xnq[:, dc * NQ + t2 * NH: dc * NQ + t2 * NH + NH],
                    s=(dc == 0), p=(dc == 15): E.matmul(o, l, r, start=s, stop=p,
                                                        skip_group_check=True),
                    inc=SPE if dc == 15 else None, waits=wl)
            t1_act = bld.op("scalar", lambda E, o=t1_bf[:], i=pT[0:16, 0:512]:
                            E.activation(o, i, AF.Copy), inc=SACT,
                            waits=[(SPE, last)])
            for hg in range(2):
                if pend_tail[0] is not None:
                    pend_tail[0]()
                    pend_tail[0] = None
                for hl in range(32):
                    hc = hg * 32 + hl
                    i2 = hc % 2
                    w = list(guard)
                    if raw_act[i2] is not None:
                        w = [(SACT, raw_act[i2])]
                    t_dr = bld.dma("sync", upraw[i2], wupq_st[hc], SW, waits=w)
                    if hc % 16 == 0:
                        wsc = list(guard) if sc_cast is None else [(SACT, sc_cast)]
                        t_ds = bld.dma("sync", scst[:], upsc_st[hc // 16], SW,
                                       waits=wsc)
                        wca = [(SW, t_ds)]
                        if scb_pe is not None:
                            wca.append((SPE, scb_pe))
                        sc_cast = bld.op(
                            "scalar", lambda E, o=scb[0:32, :], i=scst[:]:
                            E.activation(o, i, AF.Copy), inc=SACT, waits=wca)
                    wcv = [(SW, t_dr)]
                    if wup_pe[i2] is not None:
                        wcv.append((SPE, wup_pe[i2]))
                    if wup_dve[i2] is not None:
                        wcv.append((SDVE, wup_dve[i2]))
                    cvt_h = []
                    for j in range(2):
                        cvt_h.append(bld.op(
                            "scalar",
                            lambda E, o=wup_bf[i2][:, j * 1024:(j + 1) * 1024],
                            i=upraw[i2][:, j * 1024:(j + 1) * 1024]:
                            E.activation(o, i, AF.Copy, bias=-7.5), inc=SACT,
                            waits=wcv))
                    raw_act[i2] = cvt_h[1]
                    dq = None
                    dq_h = [None, None]
                    ji = hc % 16
                    for j in range(2):
                        wb = [(SACT, sc_cast), (SX, cv_esel)]
                        if pbc_dve[j] is not None:
                            wb.append((SDVE, pbc_dve[j]))
                        t_b = None
                        for ib in (2 * j, 2 * j + 1):
                            t_b = bld.op(
                                "tensor",
                                lambda E, o=pBc[:, ib * 512:(ib + 1) * 512],
                                l=esel[0:32, ji * 128:(ji + 1) * 128],
                                r=scb[0:32, ib * 512:(ib + 1) * 512]:
                                E.matmul(o, l, r, start=True, stop=True,
                                         skip_group_check=True),
                                inc=SPE if ib == 2 * j + 1 else None, waits=wb)
                        scb_pe = t_b
                        dq = bld.op(
                            "vector",
                            lambda E, o=wup_bf[i2][:, j * 1024:(j + 1) * 1024],
                            b=pBc[:, j * 1024:(j + 1) * 1024]:
                            E.tensor_tensor(o, o, b, ALU.mult),
                            inc=SDVE,
                            waits=[(SACT, cvt_h[j]), (SPE, t_b)])
                        pbc_dve[j] = dq
                        dq_h[j] = dq
                    wup_dve[i2] = dq
                    wy = [(SDVE, dq_h[0]), (SACT, t1_act), (SACT, cv_lbup)]
                    if py_cons[i2] is not None:
                        wy.append(py_cons[i2])
                    for dc in range(16):
                        if dc == 8:
                            wy = [(SDVE, dq_h[1])]
                        bld.op("tensor", lambda E, o=pY[i2][:],
                               l=wup_bf[i2][:, dc * 128:(dc + 1) * 128],
                               r=xnq[:, dc * NQ + t2 * NH: dc * NQ + t2 * NH + NH],
                               s=(dc == 0): E.matmul(o, l, r, start=s, stop=False,
                                                     skip_group_check=True),
                               waits=wy)
                    wup_pe[i2] = bld.op(
                        "tensor", lambda E, o=pY[i2][:],
                        l=lbup_bf[0:16, hc * 128:(hc + 1) * 128], r=t1_bf[0:16, :]:
                        E.matmul(o, l, r, start=False, stop=True,
                                 skip_group_check=True), inc=SPE)
                    relu_t[(t2, hg, hl)] = bld.op(
                        "vector", lambda E, o=x2g[:, hl * NH:(hl + 1) * NH],
                        i=pY[i2][:], b=bup_sb[:, hc:hc + 1]:
                        E.tensor_scalar(o, i, b, 0.0, ALU.add, ALU.max),
                        inc=SDVE, waits=[(SPE, wup_pe[i2])])
                    py_cons[i2] = (SDVE, relu_t[(t2, hg, hl)])
                if t2 == 0 and hg == 0 and "x2" in dbg:
                    bld.dma("gpsimd", dbg_t["x2"], x2g[:], SG,
                            waits=[(SDVE, relu_t[(0, 0, 31)])])
                last = None
                for hl in range(32):
                    hc = hg * 32 + hl
                    wl = [(SACT, cv_ladn), (SDVE, relu_t[(t2, hg, hl)])]
                    if hg == 0 and hl == 0:
                        wl.append((SACT, t1_act))
                    last = bld.op(
                        "tensor", lambda E, o=pT[0:16, 0:512],
                        l=ladn_bf[:, hc * 16:(hc + 1) * 16],
                        r=x2g[:, hl * NH:(hl + 1) * NH],
                        s=(hg == 0 and hl == 0), p=(hg == 1 and hl == 31):
                        E.matmul(o, l, r, start=s, stop=p, skip_group_check=True),
                        inc=SPE if (hg == 1 and hl == 31) else None, waits=wl)
                if hg == 1:
                    t2_evac = bld.op(
                        "scalar", lambda E, o=t2_bf[:], i=pT[0:16, 0:512]:
                        E.activation(o, i, AF.Copy), inc=SACT, waits=[(SPE, last)])
                    if t2 == 0 and "t2r" in dbg:
                        bld.dma("gpsimd", dbg_t["t2r"], t2_bf[:], SG,
                                waits=[(SACT, t2_evac)])
                if t2 == 0 and hg == 1 and "y2a" in dbg:
                    bld.dma("gpsimd", dbg_t["y2a"], y2acc[:], SG,
                            waits=[(SDVE, y2_t[(0, 0, 15)])])
                for fc in range(16):
                    i2 = fc % 2
                    i2d = fc % 2
                    wdr = [] if raw_act[i2d] is None else [(SACT, raw_act[i2d])]
                    t_dr = bld.dma("sync", dnraw[i2d],
                                   wdnq_st[fc][:, hg * 4096:(hg + 1) * 4096], SW,
                                   waits=wdr)
                    if fc == 0:
                        for bh in range(2):
                            wsc = [] if sc_cast is None else [(SACT, sc_cast)]
                            t_ds = bld.dma("sync", scst[:], dnsc_st[hg, bh], SW,
                                           waits=wsc)
                            wca = [(SW, t_ds)]
                            if scb_pe is not None:
                                wca.append((SPE, scb_pe))
                            sc_cast = bld.op(
                                "scalar",
                                lambda E, o=scb[bh * 32:(bh + 1) * 32, :],
                                i=scst[:]: E.activation(o, i, AF.Copy),
                                inc=SACT, waits=wca)
                    wcv = [(SW, t_dr)]
                    if wdn_pe[i2d] is not None:
                        wcv.append((SPE, wdn_pe[i2d]))
                    if wdn_deq[i2d] is not None:
                        wcv.append((SDVE, wdn_deq[i2d]))
                    wcv += dn_dbg_gate
                    dn_dbg_gate = []
                    dn_act_q = []
                    for j2 in range(4):
                        dn_act_q.append(bld.op(
                            "scalar",
                            lambda E, o=wdn_bf[i2d][:, j2 * 1024:(j2 + 1) * 1024],
                            i=dnraw[i2d][:, j2 * 1024:(j2 + 1) * 1024]:
                            E.activation(o, i, AF.Copy, bias=-7.5), inc=SACT,
                            waits=wcv))
                    raw_act[i2d] = dn_act_q[3]
                    dq = None
                    dq_q = [None] * 4
                    for j2 in range(4):
                        jj = j2 % 2
                        ji = fc * 2 + (j2 // 2)
                        wb = [(SACT, sc_cast), (SX, cv_esel)]
                        if pbc_dve[jj] is not None:
                            wb.append((SDVE, pbc_dve[jj]))
                        t_b = None
                        for half in range(2):
                            t_b = bld.op(
                                "tensor",
                                lambda E, o=pBc[:, (2 * jj + half) * 512:
                                                (2 * jj + half + 1) * 512],
                                l=esel[0:64, ji * 128:(ji + 1) * 128],
                                r=scb[0:64, jj * 1024 + half * 512:
                                      jj * 1024 + half * 512 + 512]:
                                E.matmul(o, l, r, start=True, stop=True,
                                         skip_group_check=True),
                                inc=SPE if half == 1 else None, waits=wb)
                        scb_pe = t_b
                        dq = bld.op(
                            "vector",
                            lambda E, o=wdn_bf[i2d][:, j2 * 1024:(j2 + 1) * 1024],
                            b=pBc[:, jj * 1024:(jj + 1) * 1024]:
                            E.tensor_tensor(o, o, b, ALU.mult),
                            inc=SDVE,
                            waits=[(SACT, dn_act_q[j2]), (SPE, t_b)])
                        pbc_dve[jj] = dq
                        dq_q[j2] = dq
                    wdn_deq[i2d] = dq
                    if t2 == 0 and hg == 0 and fc == 0 and "wdn0" in dbg:
                        tap = bld.dma("gpsimd", dbg_t["wdn0"], wdn_bf[0][:], SG,
                                      waits=[(SDVE, dq)])
                        dn_dbg_gate.append((SG, tap))
                    wy = [(SDVE, dq_q[0])]
                    if py_cons[i2] is not None:
                        wy.append(py_cons[i2])
                    if hg == 1:
                        wy += [(SACT, t2_evac), (SACT, cv_lbdn)]
                    y2pe = None
                    for hl in range(32):
                        if hl in (8, 16, 24):
                            wy = [(SDVE, dq_q[hl // 8])]
                        is_last = (hl == 31 and hg == 0)
                        y2pe = bld.op(
                            "tensor", lambda E, o=pY[i2][:],
                            l=wdn_bf[i2d][:, hl * 128:(hl + 1) * 128],
                            r=x2g[:, hl * NH:(hl + 1) * NH],
                            s=(hl == 0), p=is_last:
                            E.matmul(o, l, r, start=s, stop=p,
                                     skip_group_check=True),
                            inc=SPE if is_last else None, waits=wy) or y2pe
                    if hg == 1:
                        y2pe = bld.op(
                            "tensor", lambda E, o=pY[i2][:],
                            l=lbdn_bf[0:16, fc * 128:(fc + 1) * 128],
                            r=t2_bf[0:16, :]:
                            E.matmul(o, l, r, start=False, stop=True,
                                     skip_group_check=True), inc=SPE)
                    if hg == 0:
                        wacc = [(SPE, y2pe)]
                        prev_sil = sil_t.get((t2 - 1, fc))
                        if prev_sil is not None:
                            wacc.append((SACT, prev_sil))
                        y2_t[(t2, hg, fc)] = bld.op(
                            "vector", lambda E, o=y2acc[:, fc * NH:(fc + 1) * NH],
                            i=pY[i2][:]: E.tensor_copy(o, i), inc=SDVE,
                            waits=wacc)
                    else:
                        y2_t[(t2, hg, fc)] = bld.op(
                            "vector", lambda E, o=y2acc[:, fc * NH:(fc + 1) * NH],
                            b=pY[i2][:]: E.tensor_tensor(o, o, b, ALU.add),
                            inc=SDVE, waits=[(SPE, y2pe)])
                    py_cons[i2] = (SDVE, y2_t[(t2, hg, fc)])
                    wdn_pe[i2d] = y2pe
                    if pend_tail[0] is not None:
                        pend_tail[0]()
                        pend_tail[0] = None
                    if hg == 1:
                      def _tail(t2=t2, fc=fc, i2=i2):
                          wsil = [(SDVE, y2_t[(t2, 1, fc)])]
                          prev_o = odve_t.get((t2, fc - 2), odve_t.get((t2 - 1, 14 + fc)))
                          if prev_o is not None:
                              wsil.append((SDVE, prev_o))
                          sil_t[(t2, fc)] = bld.op(
                              "scalar", lambda E, o=g_tile[i2][:],
                              i=y2acc[:, fc * NH:(fc + 1) * NH],
                              b=bdn_sb[:, fc:fc + 1]:
                              E.activation(o, i, AF.Silu, bias=b), inc=SACT,
                              waits=wsil)
                          wld = list(guard)
                          if prev_o is not None:
                              wld = [(SDVE, prev_o)]
                          t_s = bld.dma("gpsimd", sg_t[i2][:],
                                        s_bnc[fc, :, t2 * NH:(t2 + 1) * NH], SG,
                                        waits=wld)
                          t_x = bld.dma("gpsimd", xq_t[i2][:],
                                        xqT[fc, :, t2 * NH:(t2 + 1) * NH], SG)
                          wmul = [(SACT, sil_t[(t2, fc)]), (SG, t_x)]
                          prev_d = outd_t.get((t2, fc - 2),
                                              outd_t.get((t2 - 1, 14 + fc)))
                          if prev_d is not None:
                              wmul.append((SG, prev_d))
                          bld.op("vector", lambda E, o=out_t[i2][:], a=sg_t[i2][:],
                                 b=g_tile[i2][:]: E.tensor_tensor(o, a, b, ALU.mult),
                                 waits=wmul)
                          odve_t[(t2, fc)] = bld.op(
                              "vector", lambda E, o=out_t[i2][:], a=out_t[i2][:],
                              b=xq_t[i2][:]: E.tensor_tensor(o, a, b, ALU.add),
                              inc=SDVE)
                          outd_t[(t2, fc)] = bld.dma(
                              "gpsimd",
                              out_d[fc * 128:(fc + 1) * 128, t2 * NH:(t2 + 1) * NH],
                              out_t[i2][:], SG, waits=[(SDVE, odve_t[(t2, fc)])])
                          if t2 == 0 and fc == 0 and "g" in dbg:
                              bld.dma("gpsimd", dbg_t["g"], g_tile[0][:], SG,
                                      waits=[(SACT, sil_t[(t2, fc)])])
                      pend_tail[0] = _tail
        if pend_tail[0] is not None:
            pend_tail[0]()
            pend_tail[0] = None
        bld.wait("gpsimd", SG, SG.n)
        bld.flush(block)
        ffn_es.close()

    es.close()
    return nc


def _rope_tables():
    inv = 10000.0 ** (-(np.arange(0, HD, 2, dtype=np.float64) / HD))
    t = np.arange(S, dtype=np.float64)
    fr = np.outer(t, inv)
    emb = np.concatenate([fr, fr], axis=1)
    return np.cos(emb), np.sin(emb)


def _stage_core(inputs, c, cos_t, sin_t):
    b, p = c // 2, c % 2
    x = np.asarray(inputs["L0_attn_x"])
    chunks = [2 * j + p for j in range(8)]
    mypos = np.concatenate([np.arange(a * 128, a * 128 + 128) for a in chunks])
    xb = x[b]
    asC = np.ascontiguousarray
    d = {}
    d["xfullT"] = asC(xb.T.reshape(16, 128, S))
    d["xqT"] = asC(xb[mypos].T.reshape(16, 128, NQ))
    Wq = np.asarray(inputs["L0_attn_Wq"])
    Wk = np.asarray(inputs["L0_attn_Wk"])
    Wv = np.asarray(inputs["L0_attn_Wv"])
    Wo = np.asarray(inputs["L0_attn_Wo"])
    d["wq_st"] = asC(Wq.T.reshape(16, 128, 16, 128).transpose(2, 1, 0, 3))
    d["wk_st"] = asC(Wk.T.reshape(16, 128, 16, 128).transpose(2, 1, 0, 3))
    wv = Wv.T.reshape(16, 128, 8, 256).transpose(2, 0, 1, 3)
    wv = wv.reshape(8, 2, 8, 128, 256).transpose(0, 1, 3, 2, 4)
    d["wv_st"] = asC(wv.reshape(8, 2, 128, 2048))
    d["wo_st"] = asC(Wo.T.reshape(16, 128, 16, 128).transpose(2, 1, 0, 3))
    d["wupq_st"] = asC(np.asarray(inputs["L0_ffn_w_up_q"]).T
                       .reshape(16, 128, 64, 128).transpose(2, 1, 0, 3)
                       .reshape(64, 128, 2048)).astype(np.int32)
    d["upsc_st"] = asC(np.asarray(inputs["L0_ffn_w_up_scale"]).T
                       .reshape(16, 2, 64, 128).transpose(2, 1, 0, 3)
                       .reshape(4, 32, 2048)).astype(np.float32)
    d["bup_st"] = asC(np.asarray(inputs["L0_ffn_b_up"]).reshape(64, 128).T
                      ).astype(np.float32)
    d["laup_st"] = asC(np.asarray(inputs["L0_ffn_w_up_lora_a"])
                       .reshape(16, 128, 16).transpose(1, 0, 2)
                       .reshape(128, 256)).astype(np.float32)
    d["lbup_st"] = asC(np.asarray(inputs["L0_ffn_w_up_lora_b"])
                       .reshape(16, 4, 2048).transpose(1, 0, 2)).astype(np.float32)
    d["wdnq_st"] = asC(np.asarray(inputs["L0_ffn_w_down_q"]).T
                       .reshape(64, 128, 16, 128).transpose(2, 1, 0, 3)
                       .reshape(16, 128, 8192)).astype(np.int32)
    dn4 = (np.asarray(inputs["L0_ffn_w_down_scale"]).T
           .reshape(64, 2, 16, 128).transpose(2, 1, 0, 3)
           .reshape(16, 2, 4, 16, 128).transpose(0, 2, 1, 3, 4)
           .reshape(16, 4, 2, 2048))
    dnb = np.stack([dn4[:, 0:2].reshape(64, 2048),
                    dn4[:, 2:4].reshape(64, 2048)])
    d["dnsc_st"] = asC(dnb.reshape(2, 2, 32, 2048)).astype(np.float32)
    ese = np.zeros((64, 32, 128), np.float32)
    for ej in range(32):
        for ep in range(128):
            ese[2 * ej + ep // 64, ej, ep] = 1.0
    d["esel_st"] = asC(ese.transpose(0, 1, 2).reshape(64, 4096).astype(BF))
    d["bdn_st"] = asC(np.asarray(inputs["L0_ffn_b_down"]).reshape(16, 128).T
                      ).astype(np.float32)
    d["ladn_st"] = asC(np.asarray(inputs["L0_ffn_w_down_lora_a"])
                       .reshape(64, 128, 16).transpose(1, 0, 2)
                       .reshape(128, 1024)).astype(np.float32)
    d["lbdn_st"] = asC(np.asarray(inputs["L0_ffn_w_down_lora_b"])).astype(np.float32)
    d["ck_st"] = asC(cos_t.T.astype(BF))
    d["sk_st"] = asC(sin_t.T.astype(BF))
    d["cq_st"] = asC((cos_t[mypos].T * SCALE).astype(BF))
    d["sq_st"] = asC((sin_t[mypos].T * SCALE).astype(BF))
    tri = np.triu(np.ones((128, 128), np.float32))
    masks = np.empty((16, 128, 128), np.float32)
    for kb in range(16):
        if kb % 2 == 0:
            masks[kb] = tri if p == 0 else 1.0
        else:
            masks[kb] = 0.0 if p == 0 else tri
    d["mask_st"] = asC(masks.transpose(1, 0, 2).astype(BF))
    rot = np.zeros((128, 128), np.float32)
    for m in range(64):
        rot[m + 64, m] = -1.0
        rot[m, m + 64] = 1.0
    d["rotm_st"] = asC(rot.astype(BF))
    sel = np.zeros((2, 128), np.float32)
    sel[0, :64] = 1.0
    sel[1, 64:] = 1.0
    d["sel_st"] = asC(sel)
    return d, b, chunks


def kernel(**inputs):
    cos_t, sin_t = _rope_tables()
    in_maps = []
    metas = []
    for c in range(8):
        d, b, chunks = _stage_core(inputs, c, cos_t, sin_t)
        in_maps.append(d)
        metas.append((b, chunks))
    nc = build_nc()
    res = run_bass_kernel_spmd(nc, in_maps, core_ids=list(range(8)))
    out = np.empty((B, S, D), np.float32)
    for c in range(8):
        b, chunks = metas[c]
        o = np.asarray(res.results[c]["out"])
        oT = o.T
        for j, a in enumerate(chunks):
            out[b, a * 128:(a + 1) * 128, :] = oT[j * 128:(j + 1) * 128, :]
    return out


# version 29
# speedup vs baseline: 1.0926x; 1.0926x over previous
import os
import numpy as np
import ml_dtypes
from contextlib import ExitStack

import concourse.bass as bass
import concourse.mybir as mybir
from concourse.bass_utils import run_bass_kernel_spmd

F32 = mybir.dt.float32
BF16 = mybir.dt.bfloat16
I32 = mybir.dt.int32
AF = mybir.ActivationFunctionType
ALU = mybir.AluOpType
BF = ml_dtypes.bfloat16

B, S, D = 4, 2048, 2048
H, HD = 16, 128
HID, R = 8192, 16
NQ = 1024
SCALE = 1.0 / np.sqrt(HD)
NH = 512

ENGS = ("sync", "tensor", "vector", "scalar", "gpsimd")
_SIM_SERIAL = os.environ.get("BASS_SIM_FP64") == "1"


class _Sem:
    __slots__ = ("h", "n")

    def __init__(self, h):
        self.h = h
        self.n = 0

    def bump(self, k):
        self.n += k
        return self.n


class _Bld:
    def __init__(self):
        self.q = {e: [] for e in ENGS}
        self.hw = {e: {} for e in ENGS}

    def wait(self, eng, sem, val):
        if sem is None or val is None or val <= 0:
            return
        if self.hw[eng].get(id(sem), 0) >= val:
            return
        self.hw[eng][id(sem)] = val
        self.q[eng].append(lambda E, s=sem, v=val: E.wait_ge(s.h, v))

    def op(self, eng, fn, inc=None, inc_by=1, waits=()):
        for s, v in waits:
            self.wait(eng, s, v)
        if inc is not None:
            tok = inc.bump(inc_by)
            self.q[eng].append(lambda E, f=fn, s=inc, v=inc_by: f(E).then_inc(s.h, v))
            return tok
        self.q[eng].append(lambda E, f=fn: f(E))
        return None

    def dma(self, queue, out, in_, sem, waits=()):
        if _SIM_SERIAL:
            waits = list(waits) + [(sem, sem.n)]
        return self.op(queue, lambda E, o=out, i=in_: E.dma_start(out=o, in_=i),
                       inc=sem, inc_by=16, waits=waits)

    def flush(self, block):
        for eng in ENGS:
            ops = self.q[eng]
            if not ops:
                continue
            self.q[eng] = []

            def body(E, ops=ops):
                for f in ops:
                    f(E)
            getattr(block, eng)(body)


def _seg(qs):
    if qs < 512:
        return [(qs, 512 - qs), (512, 512)]
    return [(qs, 1024 - qs)]


def build_nc(dbg=()):
    nc = bass.Bass("TRN2", target_bir_lowering=False)
    dt = nc.dram_tensor
    xfullT = dt("xfullT", [16, 128, S], F32, kind="ExternalInput").ap()
    xqT = dt("xqT", [16, 128, NQ], F32, kind="ExternalInput").ap()
    wq_st = dt("wq_st", [16, 128, 16, 128], F32, kind="ExternalInput").ap()
    wk_st = dt("wk_st", [16, 128, 16, 128], F32, kind="ExternalInput").ap()
    wv_st = dt("wv_st", [8, 2, 128, 2048], F32, kind="ExternalInput").ap()
    wo_st = dt("wo_st", [16, 128, 16, 128], F32, kind="ExternalInput").ap()
    wupq_st = dt("wupq_st", [64, 128, 2048], I32, kind="ExternalInput").ap()
    upsc_st = dt("upsc_st", [4, 32, 2048], F32, kind="ExternalInput").ap()
    bup_st = dt("bup_st", [128, 64], F32, kind="ExternalInput").ap()
    laup_st = dt("laup_st", [128, 256], F32, kind="ExternalInput").ap()
    lbup_st = dt("lbup_st", [4, 16, 2048], F32, kind="ExternalInput").ap()
    wdnq_st = dt("wdnq_st", [16, 128, 8192], I32, kind="ExternalInput").ap()
    dnsc_st = dt("dnsc_st", [2, 2, 32, 2048], F32, kind="ExternalInput").ap()
    esel_st = dt("esel_st", [64, 4096], BF16, kind="ExternalInput").ap()
    bdn_st = dt("bdn_st", [128, 16], F32, kind="ExternalInput").ap()
    ladn_st = dt("ladn_st", [128, 1024], F32, kind="ExternalInput").ap()
    lbdn_st = dt("lbdn_st", [16, 2048], F32, kind="ExternalInput").ap()
    ck_st = dt("ck_st", [128, S], BF16, kind="ExternalInput").ap()
    sk_st = dt("sk_st", [128, S], BF16, kind="ExternalInput").ap()
    cq_st = dt("cq_st", [128, NQ], BF16, kind="ExternalInput").ap()
    sq_st = dt("sq_st", [128, NQ], BF16, kind="ExternalInput").ap()
    mask_st = dt("mask_st", [128, 16, 128], BF16, kind="ExternalInput").ap()
    rotm_st = dt("rotm_st", [128, 128], BF16, kind="ExternalInput").ap()
    sel_st = dt("sel_st", [2, 128], F32, kind="ExternalInput").ap()

    out_d = dt("out", [D, NQ], F32, kind="ExternalOutput").ap()
    s_bnc = dt("s_bnc", [16, 128, NQ], BF16,
               kind="ExternalOutput" if "s" in dbg else "Internal").ap()
    dbg_t = {}
    for name, shape in [("xn", [128, 16 * S]), ("xnq", [128, 16 * NQ]),
                        ("kfm0", [128, S]), ("qfm0", [128, NQ]),
                        ("att", [128, 16 * NQ]), ("x2", [128, 32 * NH]),
                        ("g", [128, NH]), ("wup0", [128, 2048]),
                        ("wdn0", [128, 4096]), ("t2r", [16, NH]),
                        ("vtm0", [128, 4096]), ("rc0", [128, NQ]),
                        ("kfm7", [128, S]), ("ptb0", [128, NQ])]:
        if name in dbg:
            dbg_t[name] = dt("dbg_" + name, shape, BF16, kind="ExternalOutput").ap()
    if "y2a" in dbg:
        dbg_t["y2a"] = dt("dbg_y2a", [128, 16 * NH], F32, kind="ExternalOutput").ap()

    bld = _Bld()
    es = ExitStack()
    E = es.enter_context

    SW = _Sem(E(nc.semaphore("SW")))
    SX = _Sem(E(nc.semaphore("SX")))
    SG = _Sem(E(nc.semaphore("SG")))
    SPE = _Sem(E(nc.semaphore("SPE")))
    SACT = _Sem(E(nc.semaphore("SACT")))
    SDVE = _Sem(E(nc.semaphore("SDVE")))
    SPOOL = _Sem(E(nc.semaphore("SPOOL")))

    xnq = E(nc.sbuf_tensor("xnq", [128, 16 * NQ], BF16))
    ones_col = E(nc.sbuf_tensor("ones_col", [128, 1], BF16))
    ones_row = E(nc.sbuf_tensor("ones_row", [1, 128], F32))
    sel_sb = E(nc.sbuf_tensor("sel_sb", [2, 128], F32))
    bup_sb = E(nc.sbuf_tensor("bup_sb", [128, 64], F32))
    bdn_sb = E(nc.sbuf_tensor("bdn_sb", [128, 16], F32))
    c_eps = E(nc.sbuf_tensor("c_eps", [1, 1], F32))
    c_m75 = E(nc.sbuf_tensor("c_m75", [128, 1], F32))

    with nc.Block() as block:
        bld.dma("scalar", sel_sb[:], sel_st, SX)
        bld.dma("scalar", bup_sb[:], bup_st, SX)
        bld.dma("scalar", bdn_sb[:], bdn_st, SX)
        tok_c0 = SX.n
        bld.op("vector", lambda E: E.memset(ones_col[:], 1.0))
        bld.op("vector", lambda E: E.memset(c_eps[:], 1e-5))
        bld.op("vector", lambda E: E.memset(c_m75[:], -7.5))
        tok_ones = bld.op("vector", lambda E: E.memset(ones_row[:], 1.0), inc=SDVE,
                          waits=[(SX, tok_c0)])

        att_es = ExitStack()
        A = att_es.enter_context
        xn = A(nc.sbuf_tensor("xn", [128, 16 * S], BF16))
        ck_sb = A(nc.sbuf_tensor("ck_sb", [128, S], BF16))
        sk_sb = A(nc.sbuf_tensor("sk_sb", [128, S], BF16))
        cq_sb = A(nc.sbuf_tensor("cq_sb", [128, NQ], BF16))
        sq_sb = A(nc.sbuf_tensor("sq_sb", [128, NQ], BF16))
        mask_sb = A(nc.sbuf_tensor("mask_sb", [128, 16 * 128], BF16))
        rotm_sb = A(nc.sbuf_tensor("rotm_sb", [128, 128], BF16))
        att_all = A(nc.sbuf_tensor("att_all", [128, 16 * NQ], BF16))

        bld.dma("scalar", ck_sb[:], ck_st, SX)
        bld.dma("scalar", sk_sb[:], sk_st, SX)
        bld.dma("scalar", cq_sb[:], cq_st, SX)
        bld.dma("scalar", sq_sb[:], sq_st, SX)
        bld.dma("scalar", mask_sb[:].rearrange("p (k q) -> p k q", k=16), mask_st, SX)
        tok_cst = bld.dma("scalar", rotm_sb[:], rotm_st, SX)

        ln_es = ExitStack()
        L = ln_es.enter_context
        xst = [L(nc.sbuf_tensor(f"xst{i}", [128, S], F32)) for i in range(2)]
        xsq = [L(nc.sbuf_tensor(f"xsq{i}", [128, S], BF16)) for i in range(2)]
        mean_r = L(nc.sbuf_tensor("mean_r", [1, S], F32))
        m2_r = L(nc.sbuf_tensor("m2_r", [1, S], F32))
        rstd_r = L(nc.sbuf_tensor("rstd_r", [1, S], F32))
        meanq_r = L(nc.sbuf_tensor("meanq_r", [1, NQ], F32))
        m2q_r = L(nc.sbuf_tensor("m2q_r", [1, NQ], F32))
        rstdq_r = L(nc.sbuf_tensor("rstdq_r", [1, NQ], F32))

        ln_ps1 = ExitStack()
        pstat = ln_ps1.enter_context(nc.psum_tensor("pstat", [64, S], F32))
        pstatq = ln_ps1.enter_context(nc.psum_tensor("pstatq", [64, NQ], F32))

        tok_xbf, tok_xsq, tok_sq_mm = {}, {}, {}
        plans = [("f", xfullT, S, 4, xn, pstat), ("q", xqT, NQ, 2, xnq, pstatq)]
        for pi, (tag, src, width, ncols, dst, pst) in enumerate(plans):
            for dc in range(16):
                w = []
                pk = (tag, dc - 2) if dc >= 2 else (("f", 14 + dc) if pi == 1 else None)
                if pk is not None:
                    w = [(SDVE, tok_xbf[pk]), (SACT, tok_xsq[pk])]
                t_dma = bld.dma("sync", xst[dc % 2][:, 0:width], src[dc], SW,
                                waits=w)
                sl = dst[:, dc * width:(dc + 1) * width]
                tok_xbf[(tag, dc)] = bld.op(
                    "vector", lambda E, o=sl, i=xst[dc % 2][:, 0:width]:
                    E.tensor_copy(o, i), inc=SDVE, waits=[(SW, t_dma)])
                wsq = [(SW, t_dma)]
                if pk is not None:
                    wsq.append((SPE, tok_sq_mm[pk]))
                tok_xsq[(tag, dc)] = bld.op(
                    "scalar", lambda E, o=xsq[dc % 2][:, 0:width],
                    i=xst[dc % 2][:, 0:width]: E.activation(o, i, AF.Square),
                    inc=SACT, waits=wsq)
                for ts in range(ncols):
                    bld.op("tensor", lambda E, o=pst[0:1, ts * 512:(ts + 1) * 512],
                           l=ones_col[:], r=sl[:, ts * 512:(ts + 1) * 512],
                           s=(dc == 0): E.matmul(o, l, r, start=s, stop=False,
                                                 skip_group_check=True),
                           waits=[(SDVE, tok_xbf[(tag, dc)]), (SDVE, tok_ones)])
                last = None
                for ts in range(ncols):
                    last = bld.op(
                        "tensor", lambda E, o=pst[32:33, ts * 512:(ts + 1) * 512],
                        l=ones_col[:], r=xsq[dc % 2][:, ts * 512:(ts + 1) * 512],
                        s=(dc == 0): E.matmul(o, l, r, start=s, stop=False,
                                              skip_group_check=True),
                        inc=SPE, waits=[(SACT, tok_xsq[(tag, dc)])])
                tok_sq_mm[(tag, dc)] = last
        tok_stats = SPE.n
        for pst, n, mr, m2r, rsr in ((pstat, S, mean_r, m2_r, rstd_r),
                                     (pstatq, NQ, meanq_r, m2q_r, rstdq_r)):
            ta = bld.op("scalar", lambda E, o=mr[:], i=pst[0:1, 0:n]:
                        E.activation(o, i, AF.Copy, scale=1.0 / D), inc=SACT,
                        waits=[(SPE, tok_stats)])
            bld.op("vector", lambda E, o=m2r[:], a=mr[:]: E.tensor_mul(o, a, a),
                   waits=[(SACT, ta)])
            bld.op("vector", lambda E, o=rsr[:], i=pst[32:33, 0:n]:
                   E.tensor_scalar(o, i, 1.0 / D, None, ALU.mult))
            tv = bld.op("vector", lambda E, o=m2r[:], a=rsr[:], m=m2r[:]:
                        E.tensor_tensor(o, a, m, ALU.subtract), inc=SDVE)
            bld.op("scalar", lambda E, o=m2r[:], i=m2r[:]:
                   E.activation(o, i, AF.Ln, bias=c_eps[:]),
                   waits=[(SDVE, tv), (SDVE, tok_ones)])
            bld.op("scalar", lambda E, o=rsr[:], i=m2r[:]:
                   E.activation(o, i, AF.Exp, scale=-0.5), inc=SACT)
        tok_rows_a = SACT.n
        tok_rows_d = SDVE.n
        bld.flush(block)
        ln_ps1.close()
        ln_ps2 = ExitStack()
        pmbc = ln_ps2.enter_context(nc.psum_tensor("pmbc", [128, S], F32))
        prbc = ln_ps2.enter_context(nc.psum_tensor("prbc", [128, S], F32))
        tok_norm_d = tok_rows_d
        for tag, n, ncols, mr, rsr, dst in (("f", S, 4, mean_r, rstd_r, xn),
                                            ("q", NQ, 2, meanq_r, rstdq_r, xnq)):
            last = None
            for ts in range(ncols):
                bld.op("tensor", lambda E, o=pmbc[:, ts * 512:(ts + 1) * 512],
                       l=ones_row[:], r=mr[0:1, ts * 512:(ts + 1) * 512]:
                       E.matmul(o, l, r, start=True, stop=True, skip_group_check=True),
                       waits=[(SACT, tok_rows_a), (SDVE, tok_norm_d)])
                last = bld.op("tensor", lambda E, o=prbc[:, ts * 512:(ts + 1) * 512],
                              l=ones_row[:], r=rsr[0:1, ts * 512:(ts + 1) * 512]:
                              E.matmul(o, l, r, start=True, stop=True,
                                       skip_group_check=True), inc=SPE)
            tok_bc = last
            for dc in range(16):
                sl = dst[:, dc * n:(dc + 1) * n]
                stg = xst[dc % 2][:, 0:n]
                bld.op("vector", lambda E, o=stg, a=sl, m=pmbc[:, 0:n]:
                       E.tensor_tensor(o, a, m, ALU.subtract),
                       waits=[(SPE, tok_bc), (SACT, tok_rows_a)])
                bld.op("vector", lambda E, o=sl, a=stg, r=prbc[:, 0:n]:
                       E.tensor_tensor(o, a, r, ALU.mult), inc=SDVE)
            tok_norm_d = SDVE.n
        tok_ln_d = SDVE.n
        tok_ln_a = SACT.n
        if "xn" in dbg:
            bld.dma("gpsimd", dbg_t["xn"], xn[:], SG, waits=[(SDVE, tok_ln_d)])
        if "xnq" in dbg:
            bld.dma("gpsimd", dbg_t["xnq"], xnq[:], SG, waits=[(SDVE, tok_ln_d)])
        bld.flush(block)
        ln_ps2.close()
        ln_es.close()
        tok_ln_pe = SPE.n

        hd_es = ExitStack()
        HA = hd_es.enter_context
        v_tm = HA(nc.sbuf_tensor("v_tm", [128, 16 * 256], BF16))
        wv_bf = HA(nc.sbuf_tensor("wv_bf", [128, 4096], BF16))
        wraw = [HA(nc.sbuf_tensor(f"wraw{i}", [128, 2048], F32)) for i in range(2)]
        wr_i = [0]
        wk_bf = wv_bf[:, 0:2048]
        wq_bf = wv_bf[:, 2048:4096]
        kq_bf = HA(nc.sbuf_tensor("kq_bf", [128, 2048], BF16))
        scrA = HA(nc.sbuf_tensor("scrA", [128, 2048], BF16))
        k_fm = HA(nc.sbuf_tensor("k_fm", [128, 2048], BF16))
        q_fm = HA(nc.sbuf_tensor("q_fm", [128, NQ], BF16))
        ptb = [HA(nc.sbuf_tensor(f"ptb{i}", [128, NQ], BF16)) for i in range(3)]
        rc_f32 = HA(nc.sbuf_tensor("rc_f32", [128, NQ], F32))
        sums_sb = HA(nc.sbuf_tensor("sums_sb", [1, NQ], F32))
        pK = HA(nc.psum_tensor("pK", [128, 2048], F32))
        pQ = HA(nc.psum_tensor("pQ", [128, NQ], F32))
        pS = HA(nc.psum_tensor("pS", [128, NQ], F32))

        wraw_free = [[(SDVE, tok_ln_d), (SPE, tok_ln_pe), (SACT, tok_ln_a)],
                     [(SDVE, tok_ln_d), (SPE, tok_ln_pe), (SACT, tok_ln_a)]]
        rc_prev_gate = []
        prev_qproj = None
        v_last_pe = None
        attev_prev = None
        sums_prev = None
        rc_prev = None
        av_prev = None

        for g2 in range(8):
            cv_tok = None
            for sub in range(2):
                wi = wr_i[0] % 2
                wr_i[0] += 1
                t_dma = bld.dma("sync", wraw[wi][:], wv_st[g2, sub], SW,
                                waits=wraw_free[wi])
                wcv_v = [(SW, t_dma)]
                if prev_qproj is not None:
                    wcv_v.append((SPE, prev_qproj))
                cv_tok = bld.op("scalar",
                                lambda E, o=wv_bf[:, sub * 2048:(sub + 1) * 2048],
                                i=wraw[wi][:]: E.activation(o, i, AF.Copy),
                                inc=SACT, waits=wcv_v)
                wraw_free[wi] = [(SACT, cv_tok)]
            vp_act = {}
            for tc in range(16):
                pv = pS[:, 0:256] if tc % 2 == 0 else pS[:, 512:768]
                w = [(SACT, cv_tok), (SDVE, tok_ln_d)]
                if attev_prev is not None:
                    w += [(SDVE, attev_prev), (SACT, rc_prev)]
                if tc >= 2:
                    w.append((SACT, vp_act[tc - 2]))
                last = None
                for dc in range(16):
                    last = bld.op(
                        "tensor", lambda E, o=pv,
                        l=xn[:, dc * S + tc * 128: dc * S + tc * 128 + 128],
                        r=wv_bf[:, dc * 256:(dc + 1) * 256],
                        s=(dc == 0), p=(dc == 15): E.matmul(o, l, r, start=s, stop=p),
                        inc=SPE if dc == 15 else None, waits=w)
                vp_act[tc] = bld.op(
                    "scalar", lambda E, o=v_tm[:, tc * 256:(tc + 1) * 256], i=pv:
                    E.activation(o, i, AF.Copy), inc=SACT, waits=[(SPE, last)])
                v_last_pe = last
            tok_v_done = vp_act[15]
            if g2 == 0 and "vtm0" in dbg:
                tvt = bld.dma("gpsimd", dbg_t["vtm0"], v_tm[:], SG,
                              waits=[(SACT, tok_v_done)])
                wraw_free = wraw_free + [(SG, tvt)]

            for hh in range(2):
                h = 2 * g2 + hh
                wi = wr_i[0] % 2
                wr_i[0] += 1
                t_dma = bld.dma("sync", wraw[wi][:], wk_st[h], SW,
                                waits=wraw_free[wi])
                cvk = bld.op("scalar", lambda E, o=wk_bf, i=wraw[wi][:]:
                             E.activation(o, i, AF.Copy), inc=SACT,
                             waits=[(SW, t_dma), (SPE, v_last_pe)])
                wraw_free[wi] = [(SACT, cvk)]
                w_k = [(SACT, cvk), (SDVE, tok_ln_d)]
                if attev_prev is not None:
                    w_k += [(SDVE, attev_prev), (SACT, sums_prev)]
                tok_kproj = None
                for ts in range(4):
                    for dc in range(16):
                        tok_kproj = bld.op(
                            "tensor", lambda E, o=pK[:, ts * 512:(ts + 1) * 512],
                            l=wv_bf[:, dc * 128:dc * 128 + 128],
                            r=xn[:, dc * S + ts * 512: dc * S + ts * 512 + 512],
                            s=(dc == 0), p=(dc == 15):
                            E.matmul(o, l, r, start=s, stop=p),
                            inc=SPE if (dc == 15 and ts == 3) else None, waits=w_k)
                tok_kbf = bld.op("scalar", lambda E, o=kq_bf[:], i=pK[:]:
                                 E.activation(o, i, AF.Copy), inc=SACT,
                                 waits=[(SPE, tok_kproj)])
                tok_rotk = None
                for ts in range(4):
                    tok_rotk = bld.op(
                        "tensor", lambda E, o=pK[:, ts * 512:(ts + 1) * 512],
                        l=rotm_sb[:], r=kq_bf[:, ts * 512:(ts + 1) * 512]:
                        E.matmul(o, l, r, start=True, stop=True),
                        inc=SPE if ts == 3 else None,
                        waits=[(SACT, tok_kbf), (SX, tok_cst)])
                bld.op("vector", lambda E, o=scrA[:], a=kq_bf[:], b=ck_sb[:]:
                       E.tensor_tensor(o, a, b, ALU.mult), waits=[(SACT, tok_kbf)])
                tok_kfm = None
                for hf in range(2):
                    sl = slice(hf * 1024, (hf + 1) * 1024)
                    bld.op("vector", lambda E, o=ptb[hf][:], a=pK[:, sl],
                           b=sk_sb[:, sl]: E.tensor_tensor(o, a, b, ALU.mult),
                           waits=[(SPE, tok_rotk)])
                    tok_kfm = bld.op("vector", lambda E, o=k_fm[:, sl],
                                     a=scrA[:, sl], b=ptb[hf][:]:
                                     E.tensor_tensor(o, a, b, ALU.add),
                                     inc=SDVE if hf == 1 else None)
                wi = wr_i[0] % 2
                wr_i[0] += 1
                t_dma = bld.dma("sync", wraw[wi][:], wq_st[h], SW,
                                waits=wraw_free[wi])
                cvq = bld.op("scalar", lambda E, o=wq_bf, i=wraw[wi][:]:
                             E.activation(o, i, AF.Copy), inc=SACT,
                             waits=[(SW, t_dma), (SPE, tok_kproj)])
                wraw_free[wi] = [(SACT, cvq)]
                w_q = [(SACT, cvq), (SDVE, tok_ln_d)]
                if rc_prev is not None:
                    w_q.append((SACT, rc_prev))
                w_q += rc_prev_gate
                rc_prev_gate = []
                tok_qproj = None
                for ts in range(2):
                    for dc in range(16):
                        tok_qproj = bld.op(
                            "tensor", lambda E, o=pQ[:, ts * 512:(ts + 1) * 512],
                            l=wv_bf[:, 2048 + dc * 128:2048 + dc * 128 + 128],
                            r=xnq[:, dc * NQ + ts * 512: dc * NQ + ts * 512 + 512],
                            s=(dc == 0), p=(dc == 15):
                            E.matmul(o, l, r, start=s, stop=p),
                            inc=SPE if (dc == 15 and ts == 1) else None, waits=w_q)
                tok_qbf = bld.op("scalar", lambda E, o=kq_bf[:, 0:NQ], i=pQ[:]:
                                 E.activation(o, i, AF.Copy), inc=SACT,
                                 waits=[(SPE, tok_qproj), (SDVE, tok_kfm)])
                tok_rotq = None
                for ts in range(2):
                    tok_rotq = bld.op(
                        "tensor", lambda E, o=pQ[:, ts * 512:(ts + 1) * 512],
                        l=rotm_sb[:], r=kq_bf[:, ts * 512:(ts + 1) * 512]:
                        E.matmul(o, l, r, start=True, stop=True),
                        inc=SPE if ts == 1 else None, waits=[(SACT, tok_qbf)])
                prev_qproj = tok_qproj
                bld.op("vector", lambda E, o=scrA[:, 0:NQ], a=kq_bf[:, 0:NQ],
                       b=cq_sb[:]: E.tensor_tensor(o, a, b, ALU.mult),
                       waits=[(SACT, tok_qbf)])
                bld.op("vector", lambda E, o=ptb[0][:], a=pQ[:], b=sq_sb[:]:
                       E.tensor_tensor(o, a, b, ALU.mult), waits=[(SPE, tok_rotq)])
                tok_qfm = bld.op("vector", lambda E, o=q_fm[:], a=scrA[:, 0:NQ],
                                 b=ptb[0][:]: E.tensor_tensor(o, a, b, ALU.add),
                                 inc=SDVE)
                if h == 0 and "kfm0" in dbg:
                    bld.dma("gpsimd", dbg_t["kfm0"], k_fm[:], SG,
                            waits=[(SDVE, tok_kfm)])
                if h == 0 and "qfm0" in dbg:
                    bld.dma("gpsimd", dbg_t["qfm0"], q_fm[:], SG,
                            waits=[(SDVE, tok_qfm)])
                exp_t, mask_t, av_t = {}, {}, {}
                sums_last = None
                for step in range(18):
                    if step < 16:
                        kb = step
                        qs = 128 * (kb // 2)
                        N = NQ - qs
                        psx = pS if kb % 2 == 0 else pQ
                        w = [(SDVE, tok_qfm), (SDVE, tok_kfm), (SACT, tok_v_done)]
                        if kb >= 2:
                            w.append((SACT, exp_t[kb - 2]))
                        elif rc_prev is not None:
                            w.append((SACT, rc_prev))
                        sc_tok = None
                        segs = ((0, N),) if N <= 512 else ((0, 512), (512, N - 512))
                        for off, n in segs:
                            sc_tok = bld.op(
                                "tensor", lambda E, o=psx[:, off:off + n],
                                l=k_fm[:, kb * 128:(kb + 1) * 128],
                                r=q_fm[:, qs + off: qs + off + n]:
                                E.matmul(o, l, r, start=True, stop=True,
                                         skip_group_check=True),
                                inc=SPE if off + n == N else None, waits=w)
                        we = [(SPE, sc_tok)]
                        if kb >= 3:
                            we.append((SPE, av_t[kb - 3]))
                        elif av_prev is not None:
                            we.append((SPE, av_prev))
                        exp_t[kb] = bld.op(
                            "scalar", lambda E, o=ptb[kb % 3][:, qs:NQ],
                            i=psx[:, 0:N]: E.activation(o, i, AF.Exp), inc=SACT,
                            waits=we)
                        mask_t[kb] = bld.op(
                            "gpsimd", lambda E, o=ptb[kb % 3][:, qs:qs + 128],
                            m=mask_sb[:, kb * 128:(kb + 1) * 128]:
                            E.tensor_tensor(o, o, m, ALU.mult), inc=SPOOL,
                            waits=[(SACT, exp_t[kb])])
                    if step >= 2:
                        kb = step - 2
                        qs = 128 * (kb // 2)
                        w = [(SPOOL, mask_t[kb])]
                        for off, n in _seg(qs):
                            sums_last = bld.op(
                                "tensor",
                                lambda E, o=pK[0:1, 1024 + off: 1024 + off + n],
                                l=ones_col[:], r=ptb[kb % 3][:, off:off + n],
                                s=(kb == 0): E.matmul(o, l, r, start=s,
                                                      stop=(kb == 15),
                                                      skip_group_check=True),
                                inc=SPE if kb == 15 else None, waits=w)
                        for off, n in _seg(qs):
                            av_t[kb] = bld.op(
                                "tensor", lambda E, o=pK[:, off:off + n],
                                l=v_tm[:, kb * 256 + hh * 128:
                                       kb * 256 + hh * 128 + 128],
                                r=ptb[kb % 3][:, off:off + n],
                                s=(kb == 0): E.matmul(o, l, r, start=s,
                                                      stop=(kb == 15),
                                                      skip_group_check=True),
                                inc=SPE if kb == 15 else None, waits=w)
                t_sums = bld.op("scalar", lambda E, o=sums_sb[:],
                                i=pK[0:1, 1024:2048]: E.activation(o, i, AF.Copy),
                                inc=SACT, waits=[(SPE, sums_last)])
                t_bc = None
                for ts in range(2):
                    t_bc = bld.op(
                        "tensor", lambda E, o=pQ[:, ts * 512:(ts + 1) * 512],
                        l=ones_row[:], r=sums_sb[0:1, ts * 512:(ts + 1) * 512]:
                        E.matmul(o, l, r, start=True, stop=True,
                                 skip_group_check=True),
                        inc=SPE if ts == 1 else None,
                        waits=[(SACT, t_sums), (SACT, exp_t[15]), (SPE, av_t[15])])
                bld.op("scalar", lambda E, o=rc_f32[:], i=pQ[:]:
                       E.activation(o, i, AF.Ln), waits=[(SPE, t_bc)])
                t_rc = bld.op("scalar", lambda E, o=rc_f32[:], i=rc_f32[:]:
                              E.activation(o, i, AF.Exp, scale=-1.0), inc=SACT)
                attev_prev = bld.op(
                    "vector", lambda E, o=att_all[:, h * NQ:(h + 1) * NQ],
                    a=pK[:, 0:NQ], b=rc_f32[:]: E.tensor_tensor(o, a, b, ALU.mult),
                    inc=SDVE, waits=[(SACT, t_rc)])
                if h == 0 and "rc0" in dbg:
                    trc = bld.dma("gpsimd", dbg_t["rc0"], rc_f32[:], SG,
                                  waits=[(SACT, t_rc)])
                    rc_prev_gate.append((SG, trc))
                if h == 0 and "ptb0" in dbg:
                    tpb = bld.dma("gpsimd", dbg_t["ptb0"], ptb[15 % 3][:], SG,
                                  waits=[(SPE, av_t[15])])
                    rc_prev_gate.append((SG, tpb))
                if h == 7 and "kfm7" in dbg:
                    bld.dma("gpsimd", dbg_t["kfm7"], k_fm[:], SG,
                            waits=[(SDVE, tok_kfm)])
                sums_prev = t_sums
                rc_prev = t_rc
                av_prev = av_t[15]
        tok_att_d = attev_prev
        tok_att_a = SACT.n
        tok_att_p = SPE.n
        tok_att_pl = SPOOL.n
        dbg_gate = []
        if "att" in dbg:
            tat = bld.dma("gpsimd", dbg_t["att"], att_all[:], SG,
                          waits=[(SDVE, tok_att_d)])
            dbg_gate.append((SG, tat))
        bld.flush(block)
        hd_es.close()

        wo_es = ExitStack()
        WA = wo_es.enter_context
        woraw = [WA(nc.sbuf_tensor(f"woraw{i}", [128, 2048], F32)) for i in range(2)]
        wo_bf = [WA(nc.sbuf_tensor(f"wo_bf{i}", [128, 2048], BF16)) for i in range(2)]
        s_tile = [WA(nc.sbuf_tensor(f"s_tile{i}", [128, NQ], BF16)) for i in range(2)]
        pW = [WA(nc.psum_tensor(f"pW{i}", [128, NQ], F32)) for i in range(2)]

        wo_guard = [(SDVE, tok_att_d), (SACT, tok_att_a), (SPE, tok_att_p),
                    (SPOOL, tok_att_pl)]
        wo_cv, wo_ev, wo_dma, wo_mm = {}, {}, {}, {}
        for fc in range(16):
            w = list(wo_guard) if fc < 2 else [(SACT, wo_cv[fc - 2])]
            t_dma = bld.dma("sync", woraw[fc % 2][:], wo_st[fc], SW, waits=w)
            wcv = [(SW, t_dma)] + (list(wo_guard) if fc < 2 else
                                   [(SPE, wo_mm[fc - 2])])
            wo_cv[fc] = bld.op("scalar", lambda E, o=wo_bf[fc % 2][:],
                               i=woraw[fc % 2][:]: E.activation(o, i, AF.Copy),
                               inc=SACT, waits=wcv)
            w_mm = [(SACT, wo_cv[fc]), (SDVE, tok_att_d)]
            if fc >= 2:
                w_mm.append((SACT, wo_ev[fc - 2]))
            last = None
            for ts in range(2):
                for dc in range(16):
                    last = bld.op(
                        "tensor", lambda E, o=pW[fc % 2][:, ts * 512:(ts + 1) * 512],
                        l=wo_bf[fc % 2][:, dc * 128:(dc + 1) * 128],
                        r=att_all[:, dc * NQ + ts * 512: dc * NQ + ts * 512 + 512],
                        s=(dc == 0), p=(dc == 15): E.matmul(o, l, r, start=s, stop=p),
                        inc=SPE if (ts == 1 and dc == 15) else None, waits=w_mm)
            wo_mm[fc] = last
            wev = [(SPE, last)]
            if fc >= 2:
                wev.append((SG, wo_dma[fc - 2]))
            wo_ev[fc] = bld.op("scalar", lambda E, o=s_tile[fc % 2][:],
                               i=pW[fc % 2][:]: E.activation(o, i, AF.Copy),
                               inc=SACT, waits=wev)
            wo_dma[fc] = bld.dma("gpsimd", s_bnc[fc], s_tile[fc % 2][:], SG,
                                 waits=[(SACT, wo_ev[fc])])
        tok_wo_p = SPE.n
        tok_wo_a = SACT.n
        tok_wo_pl = SPOOL.n
        tok_wo_g = SG.n
        bld.flush(block)
        wo_es.close()
        att_es.close()

        ffn_es = ExitStack()
        FA = ffn_es.enter_context
        x2g = FA(nc.sbuf_tensor("x2g", [128, 32 * NH], BF16))
        y2acc = FA(nc.sbuf_tensor("y2acc", [128, 16 * NH], BF16))
        raw = [FA(nc.sbuf_tensor(f"raw{i}", [128, 4096], I32)) for i in range(2)]
        upraw = [raw[0][:, 0:2048], raw[1][:, 0:2048]]
        scst = FA(nc.sbuf_tensor("scst", [32, 2048], F32))
        scb = FA(nc.sbuf_tensor("scb", [64, 2048], BF16))
        esel = FA(nc.sbuf_tensor("esel", [64, 4096], BF16))
        wup_bf = [FA(nc.sbuf_tensor(f"wup_bf{i}", [128, 2048], BF16)) for i in range(2)]
        laup_bf = FA(nc.sbuf_tensor("laup_bf", [128, 256], BF16))
        lbup_bf = FA(nc.sbuf_tensor("lbup_bf", [16, 8192], BF16))
        ladn_bf = FA(nc.sbuf_tensor("ladn_bf", [128, 1024], BF16))
        lbdn_bf = FA(nc.sbuf_tensor("lbdn_bf", [16, 2048], BF16))
        t1_bf = FA(nc.sbuf_tensor("t1_bf", [16, NH], BF16))
        t2_bf = FA(nc.sbuf_tensor("t2_bf", [16, NH], BF16))
        g_tile = [FA(nc.sbuf_tensor(f"g_tile{i}", [128, NH], BF16)) for i in range(2)]
        sg_t = [FA(nc.sbuf_tensor(f"sg_t{i}", [128, NH], BF16)) for i in range(2)]
        xq_t = [FA(nc.sbuf_tensor(f"xq_t{i}", [128, NH], F32)) for i in range(2)]
        out_t = [FA(nc.sbuf_tensor(f"out_t{i}", [128, NH], F32)) for i in range(2)]
        pBc = FA(nc.psum_tensor("pBc", [128, 2048], F32))
        pY = [FA(nc.psum_tensor(f"pY{i}", [128, 512], F32)) for i in range(2)]
        pT = FA(nc.psum_tensor("pT", [16, 512], F32))

        guard = [(SPE, tok_wo_p), (SACT, tok_wo_a), (SPOOL, tok_wo_pl),
                 (SDVE, tok_att_d), (SG, tok_wo_g)] + dbg_gate
        t_e = bld.dma("scalar", esel[:], esel_st, SX, waits=guard)
        cv_esel = t_e
        lora_es = ExitStack()
        laup_f = lora_es.enter_context(nc.sbuf_tensor("laup_f", [128, 256], F32))
        lb_f = lora_es.enter_context(nc.sbuf_tensor("lb_f", [16, 2048], F32))
        ladn_f = lora_es.enter_context(nc.sbuf_tensor("ladn_f", [128, 1024], F32))
        t_d = bld.dma("scalar", laup_f[:], laup_st, SX, waits=guard)
        cv_laup = bld.op("scalar", lambda E, o=laup_bf[:], i=laup_f[:]:
                         E.activation(o, i, AF.Copy), inc=SACT,
                         waits=guard + [(SX, t_d)])
        cv_lbup = cv_laup
        for i in range(4):
            t_d = bld.dma("scalar", lb_f[:], lbup_st[i], SX,
                          waits=[(SACT, cv_lbup)])
            cv_lbup = bld.op("scalar",
                             lambda E, o=lbup_bf[:, i * 2048:(i + 1) * 2048],
                             i_=lb_f[:]: E.activation(o, i_, AF.Copy), inc=SACT,
                             waits=[(SX, t_d)])
        t_d = bld.dma("scalar", ladn_f[:], ladn_st, SX, waits=guard)
        cv_ladn = bld.op("scalar", lambda E, o=ladn_bf[:], i=ladn_f[:]:
                         E.activation(o, i, AF.Copy), inc=SACT, waits=[(SX, t_d)])
        t_d = bld.dma("scalar", lb_f[:], lbdn_st, SX, waits=[(SACT, cv_lbup)])
        cv_lbdn = bld.op("scalar", lambda E, o=lbdn_bf[:], i=lb_f[:]:
                         E.activation(o, i, AF.Copy), inc=SACT, waits=[(SX, t_d)])
        bld.flush(block)
        lora_es.close()
        dnraw = [raw[0][:], raw[1][:]]
        wdn_bf = [FA(nc.sbuf_tensor(f"wdn_bf{i}", [128, 4096], BF16)) for i in range(2)]

        py_cons = [None, None]
        raw_act = [None, None]
        pbc_dve = [None] * 4
        wup_dve = [None, None]
        sc_cast = None
        scb_pe = None
        wup_pe = [None, None]

        wdn_pe = [None, None]
        wdn_deq = [None, None]
        dn_bc_last = None
        t1_act = None
        t2_evac = None
        sil_t, odve_t, outd_t, relu_t, y2_t = {}, {}, {}, {}, {}
        pend_tail = [None]
        dn_dbg_gate = []

        for t2 in range(2):
            if pend_tail[0] is not None:
                pend_tail[0]()
                pend_tail[0] = None
            last = None
            for dc in range(16):
                wl = list(guard) + [(SACT, cv_laup)]
                if t2_evac is not None:
                    wl.append((SACT, t2_evac))
                last = bld.op(
                    "tensor", lambda E, o=pT[0:16, 0:512],
                    l=laup_bf[:, dc * 16:(dc + 1) * 16],
                    r=xnq[:, dc * NQ + t2 * NH: dc * NQ + t2 * NH + NH],
                    s=(dc == 0), p=(dc == 15): E.matmul(o, l, r, start=s, stop=p,
                                                        skip_group_check=True),
                    inc=SPE if dc == 15 else None, waits=wl)
            t1_act = bld.op("scalar", lambda E, o=t1_bf[:], i=pT[0:16, 0:512]:
                            E.activation(o, i, AF.Copy), inc=SACT,
                            waits=[(SPE, last)])
            for hg in range(2):
                if pend_tail[0] is not None:
                    pend_tail[0]()
                    pend_tail[0] = None
                for hl in range(32):
                    hc = hg * 32 + hl
                    i2 = hc % 2
                    w = list(guard)
                    if raw_act[i2] is not None:
                        w = [(SACT, raw_act[i2])]
                    t_dr = bld.dma("sync", upraw[i2], wupq_st[hc], SW, waits=w)
                    if hc % 16 == 0:
                        wsc = list(guard) if sc_cast is None else [(SACT, sc_cast)]
                        t_ds = bld.dma("sync", scst[:], upsc_st[hc // 16], SW,
                                       waits=wsc)
                        wca = [(SW, t_ds)]
                        if scb_pe is not None:
                            wca.append((SPE, scb_pe))
                        sc_cast = bld.op(
                            "scalar", lambda E, o=scb[0:32, :], i=scst[:]:
                            E.activation(o, i, AF.Copy), inc=SACT, waits=wca)
                    wcv = [(SW, t_dr)]
                    if wup_pe[i2] is not None:
                        wcv.append((SPE, wup_pe[i2]))
                    if wup_dve[i2] is not None:
                        wcv.append((SDVE, wup_dve[i2]))
                    cvt_h = []
                    for j in range(2):
                        cvt_h.append(bld.op(
                            "scalar",
                            lambda E, o=wup_bf[i2][:, j * 1024:(j + 1) * 1024],
                            i=upraw[i2][:, j * 1024:(j + 1) * 1024]:
                            E.activation(o, i, AF.Copy, bias=-7.5), inc=SACT,
                            waits=wcv))
                    raw_act[i2] = cvt_h[1]
                    dq = None
                    dq_q4 = [None] * 4
                    ji = hc % 16
                    for ib in range(4):
                        wb = [(SACT, sc_cast), (SX, cv_esel)]
                        if pbc_dve[ib] is not None:
                            wb.append((SDVE, pbc_dve[ib]))
                        t_b = bld.op(
                            "tensor",
                            lambda E, o=pBc[:, ib * 512:(ib + 1) * 512],
                            l=esel[0:32, ji * 128:(ji + 1) * 128],
                            r=scb[0:32, ib * 512:(ib + 1) * 512]:
                            E.matmul(o, l, r, start=True, stop=True,
                                     skip_group_check=True),
                            inc=SPE, waits=wb)
                        scb_pe = t_b
                        dq = bld.op(
                            "vector",
                            lambda E, o=wup_bf[i2][:, ib * 512:(ib + 1) * 512],
                            b=pBc[:, ib * 512:(ib + 1) * 512]:
                            E.tensor_tensor(o, o, b, ALU.mult),
                            inc=SDVE,
                            waits=[(SACT, cvt_h[ib // 2]), (SPE, t_b)])
                        pbc_dve[ib] = dq
                        dq_q4[ib] = dq
                    wup_dve[i2] = dq
                    wy = [(SDVE, dq_q4[0]), (SACT, t1_act), (SACT, cv_lbup)]
                    if py_cons[i2] is not None:
                        wy.append(py_cons[i2])
                    for dc in range(16):
                        if dc in (4, 8, 12):
                            wy = [(SDVE, dq_q4[dc // 4])]
                        bld.op("tensor", lambda E, o=pY[i2][:],
                               l=wup_bf[i2][:, dc * 128:(dc + 1) * 128],
                               r=xnq[:, dc * NQ + t2 * NH: dc * NQ + t2 * NH + NH],
                               s=(dc == 0): E.matmul(o, l, r, start=s, stop=False,
                                                     skip_group_check=True),
                               waits=wy)
                    wup_pe[i2] = bld.op(
                        "tensor", lambda E, o=pY[i2][:],
                        l=lbup_bf[0:16, hc * 128:(hc + 1) * 128], r=t1_bf[0:16, :]:
                        E.matmul(o, l, r, start=False, stop=True,
                                 skip_group_check=True), inc=SPE)
                    relu_t[(t2, hg, hl)] = bld.op(
                        "vector", lambda E, o=x2g[:, hl * NH:(hl + 1) * NH],
                        i=pY[i2][:], b=bup_sb[:, hc:hc + 1]:
                        E.tensor_scalar(o, i, b, 0.0, ALU.add, ALU.max),
                        inc=SDVE, waits=[(SPE, wup_pe[i2])])
                    py_cons[i2] = (SDVE, relu_t[(t2, hg, hl)])
                if t2 == 0 and hg == 0 and "x2" in dbg:
                    bld.dma("gpsimd", dbg_t["x2"], x2g[:], SG,
                            waits=[(SDVE, relu_t[(0, 0, 31)])])
                last = None
                for hl in range(32):
                    hc = hg * 32 + hl
                    wl = [(SACT, cv_ladn), (SDVE, relu_t[(t2, hg, hl)])]
                    if hg == 0 and hl == 0:
                        wl.append((SACT, t1_act))
                    last = bld.op(
                        "tensor", lambda E, o=pT[0:16, 0:512],
                        l=ladn_bf[:, hc * 16:(hc + 1) * 16],
                        r=x2g[:, hl * NH:(hl + 1) * NH],
                        s=(hg == 0 and hl == 0), p=(hg == 1 and hl == 31):
                        E.matmul(o, l, r, start=s, stop=p, skip_group_check=True),
                        inc=SPE if (hg == 1 and hl == 31) else None, waits=wl)
                if hg == 1:
                    t2_evac = bld.op(
                        "scalar", lambda E, o=t2_bf[:], i=pT[0:16, 0:512]:
                        E.activation(o, i, AF.Copy), inc=SACT, waits=[(SPE, last)])
                    if t2 == 0 and "t2r" in dbg:
                        bld.dma("gpsimd", dbg_t["t2r"], t2_bf[:], SG,
                                waits=[(SACT, t2_evac)])
                if t2 == 0 and hg == 1 and "y2a" in dbg:
                    bld.dma("gpsimd", dbg_t["y2a"], y2acc[:], SG,
                            waits=[(SDVE, y2_t[(0, 0, 15)])])
                for fc in range(16):
                    i2 = fc % 2
                    i2d = fc % 2
                    wdr = [] if raw_act[i2d] is None else [(SACT, raw_act[i2d])]
                    t_dr = bld.dma("sync", dnraw[i2d],
                                   wdnq_st[fc][:, hg * 4096:(hg + 1) * 4096], SW,
                                   waits=wdr)
                    if fc == 0:
                        for bh in range(2):
                            wsc = [] if sc_cast is None else [(SACT, sc_cast)]
                            t_ds = bld.dma("sync", scst[:], dnsc_st[hg, bh], SW,
                                           waits=wsc)
                            wca = [(SW, t_ds)]
                            if scb_pe is not None:
                                wca.append((SPE, scb_pe))
                            sc_cast = bld.op(
                                "scalar",
                                lambda E, o=scb[bh * 32:(bh + 1) * 32, :],
                                i=scst[:]: E.activation(o, i, AF.Copy),
                                inc=SACT, waits=wca)
                    wcv = [(SW, t_dr)]
                    if wdn_pe[i2d] is not None:
                        wcv.append((SPE, wdn_pe[i2d]))
                    if wdn_deq[i2d] is not None:
                        wcv.append((SDVE, wdn_deq[i2d]))
                    wcv += dn_dbg_gate
                    dn_dbg_gate = []
                    dn_act_q = []
                    for j2 in range(4):
                        dn_act_q.append(bld.op(
                            "scalar",
                            lambda E, o=wdn_bf[i2d][:, j2 * 1024:(j2 + 1) * 1024],
                            i=dnraw[i2d][:, j2 * 1024:(j2 + 1) * 1024]:
                            E.activation(o, i, AF.Copy, bias=-7.5), inc=SACT,
                            waits=wcv))
                    raw_act[i2d] = dn_act_q[3]
                    dq = None
                    dq_e = [None] * 8
                    for j2 in range(4):
                        jj = j2 % 2
                        ji = fc * 2 + (j2 // 2)
                        for half in range(2):
                            rb = 2 * jj + half
                            ee = 2 * j2 + half
                            wb = [(SACT, sc_cast), (SX, cv_esel)]
                            if pbc_dve[rb] is not None:
                                wb.append((SDVE, pbc_dve[rb]))
                            t_b = bld.op(
                                "tensor",
                                lambda E, o=pBc[:, rb * 512:(rb + 1) * 512],
                                l=esel[0:64, ji * 128:(ji + 1) * 128],
                                r=scb[0:64, jj * 1024 + half * 512:
                                      jj * 1024 + half * 512 + 512]:
                                E.matmul(o, l, r, start=True, stop=True,
                                         skip_group_check=True),
                                inc=SPE, waits=wb)
                            scb_pe = t_b
                            dq = bld.op(
                                "vector",
                                lambda E,
                                o=wdn_bf[i2d][:, ee * 512:(ee + 1) * 512],
                                b=pBc[:, rb * 512:(rb + 1) * 512]:
                                E.tensor_tensor(o, o, b, ALU.mult),
                                inc=SDVE,
                                waits=[(SACT, dn_act_q[j2]), (SPE, t_b)])
                            pbc_dve[rb] = dq
                            dq_e[ee] = dq
                    wdn_deq[i2d] = dq
                    if t2 == 0 and hg == 0 and fc == 0 and "wdn0" in dbg:
                        tap = bld.dma("gpsimd", dbg_t["wdn0"], wdn_bf[0][:], SG,
                                      waits=[(SDVE, dq)])
                        dn_dbg_gate.append((SG, tap))
                    wy = [(SDVE, dq_e[0])]
                    if py_cons[i2] is not None:
                        wy.append(py_cons[i2])
                    if hg == 1:
                        wy += [(SACT, t2_evac), (SACT, cv_lbdn)]
                    y2pe = None
                    for hl in range(32):
                        if hl % 4 == 0 and hl > 0:
                            wy = [(SDVE, dq_e[hl // 4])]
                        is_last = (hl == 31 and hg == 0)
                        y2pe = bld.op(
                            "tensor", lambda E, o=pY[i2][:],
                            l=wdn_bf[i2d][:, hl * 128:(hl + 1) * 128],
                            r=x2g[:, hl * NH:(hl + 1) * NH],
                            s=(hl == 0), p=is_last:
                            E.matmul(o, l, r, start=s, stop=p,
                                     skip_group_check=True),
                            inc=SPE if is_last else None, waits=wy) or y2pe
                    if hg == 1:
                        y2pe = bld.op(
                            "tensor", lambda E, o=pY[i2][:],
                            l=lbdn_bf[0:16, fc * 128:(fc + 1) * 128],
                            r=t2_bf[0:16, :]:
                            E.matmul(o, l, r, start=False, stop=True,
                                     skip_group_check=True), inc=SPE)
                    if hg == 0:
                        wacc = [(SPE, y2pe)]
                        prev_sil = sil_t.get((t2 - 1, fc))
                        if prev_sil is not None:
                            wacc.append((SACT, prev_sil))
                        y2_t[(t2, hg, fc)] = bld.op(
                            "vector", lambda E, o=y2acc[:, fc * NH:(fc + 1) * NH],
                            i=pY[i2][:]: E.tensor_copy(o, i), inc=SDVE,
                            waits=wacc)
                    else:
                        y2_t[(t2, hg, fc)] = bld.op(
                            "vector", lambda E, o=y2acc[:, fc * NH:(fc + 1) * NH],
                            b=pY[i2][:]: E.tensor_tensor(o, o, b, ALU.add),
                            inc=SDVE, waits=[(SPE, y2pe)])
                    py_cons[i2] = (SDVE, y2_t[(t2, hg, fc)])
                    wdn_pe[i2d] = y2pe
                    if pend_tail[0] is not None:
                        pend_tail[0]()
                        pend_tail[0] = None
                    if hg == 1:
                      def _tail(t2=t2, fc=fc, i2=i2):
                          wsil = [(SDVE, y2_t[(t2, 1, fc)])]
                          prev_o = odve_t.get((t2, fc - 2), odve_t.get((t2 - 1, 14 + fc)))
                          if prev_o is not None:
                              wsil.append((SDVE, prev_o))
                          sil_t[(t2, fc)] = bld.op(
                              "scalar", lambda E, o=g_tile[i2][:],
                              i=y2acc[:, fc * NH:(fc + 1) * NH],
                              b=bdn_sb[:, fc:fc + 1]:
                              E.activation(o, i, AF.Silu, bias=b), inc=SACT,
                              waits=wsil)
                          wld = list(guard)
                          if prev_o is not None:
                              wld = [(SDVE, prev_o)]
                          t_s = bld.dma("gpsimd", sg_t[i2][:],
                                        s_bnc[fc, :, t2 * NH:(t2 + 1) * NH], SG,
                                        waits=wld)
                          t_x = bld.dma("gpsimd", xq_t[i2][:],
                                        xqT[fc, :, t2 * NH:(t2 + 1) * NH], SG)
                          wmul = [(SACT, sil_t[(t2, fc)]), (SG, t_x)]
                          prev_d = outd_t.get((t2, fc - 2),
                                              outd_t.get((t2 - 1, 14 + fc)))
                          if prev_d is not None:
                              wmul.append((SG, prev_d))
                          bld.op("vector", lambda E, o=out_t[i2][:], a=sg_t[i2][:],
                                 b=g_tile[i2][:]: E.tensor_tensor(o, a, b, ALU.mult),
                                 waits=wmul)
                          odve_t[(t2, fc)] = bld.op(
                              "vector", lambda E, o=out_t[i2][:], a=out_t[i2][:],
                              b=xq_t[i2][:]: E.tensor_tensor(o, a, b, ALU.add),
                              inc=SDVE)
                          outd_t[(t2, fc)] = bld.dma(
                              "gpsimd",
                              out_d[fc * 128:(fc + 1) * 128, t2 * NH:(t2 + 1) * NH],
                              out_t[i2][:], SG, waits=[(SDVE, odve_t[(t2, fc)])])
                          if t2 == 0 and fc == 0 and "g" in dbg:
                              bld.dma("gpsimd", dbg_t["g"], g_tile[0][:], SG,
                                      waits=[(SACT, sil_t[(t2, fc)])])
                      pend_tail[0] = _tail
        if pend_tail[0] is not None:
            pend_tail[0]()
            pend_tail[0] = None
        bld.wait("gpsimd", SG, SG.n)
        bld.flush(block)
        ffn_es.close()

    es.close()
    return nc


def _rope_tables():
    inv = 10000.0 ** (-(np.arange(0, HD, 2, dtype=np.float64) / HD))
    t = np.arange(S, dtype=np.float64)
    fr = np.outer(t, inv)
    emb = np.concatenate([fr, fr], axis=1)
    return np.cos(emb), np.sin(emb)


def _stage_core(inputs, c, cos_t, sin_t):
    b, p = c // 2, c % 2
    x = np.asarray(inputs["L0_attn_x"])
    chunks = [2 * j + p for j in range(8)]
    mypos = np.concatenate([np.arange(a * 128, a * 128 + 128) for a in chunks])
    xb = x[b]
    asC = np.ascontiguousarray
    d = {}
    d["xfullT"] = asC(xb.T.reshape(16, 128, S))
    d["xqT"] = asC(xb[mypos].T.reshape(16, 128, NQ))
    Wq = np.asarray(inputs["L0_attn_Wq"])
    Wk = np.asarray(inputs["L0_attn_Wk"])
    Wv = np.asarray(inputs["L0_attn_Wv"])
    Wo = np.asarray(inputs["L0_attn_Wo"])
    d["wq_st"] = asC(Wq.T.reshape(16, 128, 16, 128).transpose(2, 1, 0, 3))
    d["wk_st"] = asC(Wk.T.reshape(16, 128, 16, 128).transpose(2, 1, 0, 3))
    wv = Wv.T.reshape(16, 128, 8, 256).transpose(2, 0, 1, 3)
    wv = wv.reshape(8, 2, 8, 128, 256).transpose(0, 1, 3, 2, 4)
    d["wv_st"] = asC(wv.reshape(8, 2, 128, 2048))
    d["wo_st"] = asC(Wo.T.reshape(16, 128, 16, 128).transpose(2, 1, 0, 3))
    d["wupq_st"] = asC(np.asarray(inputs["L0_ffn_w_up_q"]).T
                       .reshape(16, 128, 64, 128).transpose(2, 1, 0, 3)
                       .reshape(64, 128, 2048)).astype(np.int32)
    d["upsc_st"] = asC(np.asarray(inputs["L0_ffn_w_up_scale"]).T
                       .reshape(16, 2, 64, 128).transpose(2, 1, 0, 3)
                       .reshape(4, 32, 2048)).astype(np.float32)
    d["bup_st"] = asC(np.asarray(inputs["L0_ffn_b_up"]).reshape(64, 128).T
                      ).astype(np.float32)
    d["laup_st"] = asC(np.asarray(inputs["L0_ffn_w_up_lora_a"])
                       .reshape(16, 128, 16).transpose(1, 0, 2)
                       .reshape(128, 256)).astype(np.float32)
    d["lbup_st"] = asC(np.asarray(inputs["L0_ffn_w_up_lora_b"])
                       .reshape(16, 4, 2048).transpose(1, 0, 2)).astype(np.float32)
    d["wdnq_st"] = asC(np.asarray(inputs["L0_ffn_w_down_q"]).T
                       .reshape(64, 128, 16, 128).transpose(2, 1, 0, 3)
                       .reshape(16, 128, 8192)).astype(np.int32)
    dn4 = (np.asarray(inputs["L0_ffn_w_down_scale"]).T
           .reshape(64, 2, 16, 128).transpose(2, 1, 0, 3)
           .reshape(16, 2, 4, 16, 128).transpose(0, 2, 1, 3, 4)
           .reshape(16, 4, 2, 2048))
    dnb = np.stack([dn4[:, 0:2].reshape(64, 2048),
                    dn4[:, 2:4].reshape(64, 2048)])
    d["dnsc_st"] = asC(dnb.reshape(2, 2, 32, 2048)).astype(np.float32)
    ese = np.zeros((64, 32, 128), np.float32)
    for ej in range(32):
        for ep in range(128):
            ese[2 * ej + ep // 64, ej, ep] = 1.0
    d["esel_st"] = asC(ese.transpose(0, 1, 2).reshape(64, 4096).astype(BF))
    d["bdn_st"] = asC(np.asarray(inputs["L0_ffn_b_down"]).reshape(16, 128).T
                      ).astype(np.float32)
    d["ladn_st"] = asC(np.asarray(inputs["L0_ffn_w_down_lora_a"])
                       .reshape(64, 128, 16).transpose(1, 0, 2)
                       .reshape(128, 1024)).astype(np.float32)
    d["lbdn_st"] = asC(np.asarray(inputs["L0_ffn_w_down_lora_b"])).astype(np.float32)
    d["ck_st"] = asC(cos_t.T.astype(BF))
    d["sk_st"] = asC(sin_t.T.astype(BF))
    d["cq_st"] = asC((cos_t[mypos].T * SCALE).astype(BF))
    d["sq_st"] = asC((sin_t[mypos].T * SCALE).astype(BF))
    tri = np.triu(np.ones((128, 128), np.float32))
    masks = np.empty((16, 128, 128), np.float32)
    for kb in range(16):
        if kb % 2 == 0:
            masks[kb] = tri if p == 0 else 1.0
        else:
            masks[kb] = 0.0 if p == 0 else tri
    d["mask_st"] = asC(masks.transpose(1, 0, 2).astype(BF))
    rot = np.zeros((128, 128), np.float32)
    for m in range(64):
        rot[m + 64, m] = -1.0
        rot[m, m + 64] = 1.0
    d["rotm_st"] = asC(rot.astype(BF))
    sel = np.zeros((2, 128), np.float32)
    sel[0, :64] = 1.0
    sel[1, 64:] = 1.0
    d["sel_st"] = asC(sel)
    return d, b, chunks


def kernel(**inputs):
    cos_t, sin_t = _rope_tables()
    in_maps = []
    metas = []
    for c in range(8):
        d, b, chunks = _stage_core(inputs, c, cos_t, sin_t)
        in_maps.append(d)
        metas.append((b, chunks))
    nc = build_nc()
    res = run_bass_kernel_spmd(nc, in_maps, core_ids=list(range(8)))
    out = np.empty((B, S, D), np.float32)
    for c in range(8):
        b, chunks = metas[c]
        o = np.asarray(res.results[c]["out"])
        oT = o.T
        for j, a in enumerate(chunks):
            out[b, a * 128:(a + 1) * 128, :] = oT[j * 128:(j + 1) * 128, :]
    return out
